# version 13
# baseline (speedup 1.0000x reference)
import numpy as np
import ml_dtypes
import concourse.bass as bass
import concourse.mybir as mybir
from concourse.bass_utils import run_bass_kernel_spmd
from contextlib import ExitStack

F32 = mybir.dt.float32
BF16 = mybir.dt.bfloat16
I32 = mybir.dt.int32
AF = mybir.ActivationFunctionType
ALU = mybir.AluOpType
AX = mybir.AxisListType

T = 2048
D = 2048
NH = 16
HD = 128
II = 8192
TC = 16
DCH = 16
NIC = 64
BLK = 512
NBLK = 4
NSTRIP = 9
INVSQ = float(np.float32(1.0) / np.sqrt(np.float32(128.0)))
CBIAS = -15.0
EPS = 1e-5

ENGS = ("sync", "gpsimd", "tensor", "vector", "scalar")


class Sched:
    def __init__(self):
        self.ops = {k: [] for k in ENGS}
        self.n = {}

    def cur(self, sem):
        return self.n.get(sem.name, 0)

    def _bump(self, sem, by):
        v = self.n.get(sem.name, 0) + by
        self.n[sem.name] = v
        return v

    def emit(self, eng, fn):
        self.ops[eng].append(fn)

    def wait(self, eng, sem, val):
        if val is None or val <= 0:
            return
        self.emit(eng, lambda e, s=sem, v=val: e.wait_ge(s, v))

    def dma(self, eng, sem, dst, src):
        v = self._bump(sem, 16)
        self.emit(eng, lambda e, d=dst, s=src, sm=sem: e.dma_start(d, s).then_inc(sm, 16))
        return v

    def op(self, eng, build, sem=None):
        if sem is None:
            self.emit(eng, lambda e, b=build: b(e))
            return None
        v = self._bump(sem, 1)
        self.emit(eng, lambda e, b=build, sm=sem: b(e).then_inc(sm, 1))
        return v

    def fwd(self, eng, qsem, qval, fsem):
        self.wait(eng, qsem, qval)
        v = self._bump(fsem, 1)
        self.emit(eng, lambda e, s=fsem: e.sem_inc(s, 1))
        return v


def build_program(debug=False):
    nc = bass.Bass("TRN2", target_bir_lowering=False)

    x_d = nc.dram_tensor("x", [T, D], F32, kind="ExternalInput").ap()
    wattn_d = nc.dram_tensor("w_attn", [D, 2304], F32, kind="ExternalInput").ap()
    wout_d = nc.dram_tensor("w_out", [D, D], F32, kind="ExternalInput").ap()
    wg_i_d = nc.dram_tensor("wg_img_i", [32, 128, 4096], I32, kind="ExternalInput").ap()
    wu_i_d = nc.dram_tensor("wu_img_i", [32, 128, 4096], I32, kind="ExternalInput").ap()
    wd_i_d = nc.dram_tensor("wd_img_i", [32, 128, 4096], I32, kind="ExternalInput").ap()
    gs_d = nc.dram_tensor("gscale", [128, 64], F32, kind="ExternalInput").ap()
    us_d = nc.dram_tensor("uscale", [128, 64], F32, kind="ExternalInput").ap()
    ds_d = nc.dram_tensor("dscale", [128, 16], F32, kind="ExternalInput").ap()
    cos_d = nc.dram_tensor("cos_t", [128, T], F32, kind="ExternalInput").ap()
    sin_d = nc.dram_tensor("sin_t", [128, T], F32, kind="ExternalInput").ap()
    ident_d = nc.dram_tensor("ident", [128, 128], BF16, kind="ExternalInput").ap()
    onescol_d = nc.dram_tensor("ones_col", [128, 1], BF16, kind="ExternalInput").ap()
    onesrow_d = nc.dram_tensor("ones_row", [1, 128], F32, kind="ExternalInput").ap()

    out_d = nc.dram_tensor("out", [T, D], F32, kind="ExternalOutput").ap()

    iknd = "ExternalOutput" if debug else "Internal"
    xn_sp = nc.dram_tensor("xn_sp", [D, T], BF16, kind=iknd).ap()
    s_sp = nc.dram_tensor("s_sp", [D, T], BF16, kind=iknd).ap()
    wg_b_d = nc.dram_tensor("wg_img_b", [32, 128, 4096], BF16, kind=iknd).ap()
    wu_b_d = nc.dram_tensor("wu_img_b", [32, 128, 4096], BF16, kind=iknd).ap()
    wd_b_d = nc.dram_tensor("wd_img_b", [32, 128, 4096], BF16, kind=iknd).ap()
    if debug == 3:
        dbg_ga = nc.dram_tensor("dbg_ga", [4, 128, 1024], F32, kind="ExternalOutput").ap()
        dbg_cb = nc.dram_tensor("dbg_cb", [4, 128, 8192], BF16, kind="ExternalOutput").ap()
        dbg_hm = nc.dram_tensor("dbg_hm", [128, 32768], BF16, kind="ExternalOutput").ap()
        dbg_sb = nc.dram_tensor("dbg_sb", [4, 128, 8192], BF16, kind="ExternalOutput").ap()
    if debug is True:
        dbg_q = nc.dram_tensor("dbg_q", [128, 32768], BF16, kind="ExternalOutput").ap()
        dbg_k = nc.dram_tensor("dbg_k", [128, T], BF16, kind="ExternalOutput").ap()
        dbg_v = nc.dram_tensor("dbg_v", [128, T], BF16, kind="ExternalOutput").ap()
        dbg_av = nc.dram_tensor("dbg_av", [128, 32768], BF16, kind="ExternalOutput").ap()
        dbg_hm = nc.dram_tensor("dbg_hm", [128, 32768], BF16, kind="ExternalOutput").ap()

    S = Sched()

    with ExitStack() as top:
        e = top.enter_context
        qS = e(nc.semaphore("qS"))
        qG = e(nc.semaphore("qG"))
        qA = e(nc.semaphore("qA"))
        qB = e(nc.semaphore("qB"))
        qC = e(nc.semaphore("qC"))
        qD = e(nc.semaphore("qD"))
        qE = e(nc.semaphore("qE"))
        qF = e(nc.semaphore("qF"))
        qH = e(nc.semaphore("qH"))
        qO = e(nc.semaphore("qO"))
        fS = e(nc.semaphore("fS"))
        fG = e(nc.semaphore("fG"))
        tE = e(nc.semaphore("tE"))
        vE = e(nc.semaphore("vE"))
        sE = e(nc.semaphore("sE"))

        big_a = e(nc.sbuf_tensor("big_a", [128, 32768], BF16))
        big_b = e(nc.sbuf_tensor("big_b", [128, 32768], BF16))
        k_sb = e(nc.sbuf_tensor("k_sb", [128, T], BF16))
        v_sb = e(nc.sbuf_tensor("v_sb", [128, T], BF16))
        ident = e(nc.sbuf_tensor("ident_s", [128, 128], BF16))
        ones_c = e(nc.sbuf_tensor("ones_c", [128, 1], BF16))
        ones_r = e(nc.sbuf_tensor("ones_r", [1, 128], F32))
        gs_sb = e(nc.sbuf_tensor("gs_sb", [128, 64], F32))
        us_sb = e(nc.sbuf_tensor("us_sb", [128, 64], F32))
        ds_sb = e(nc.sbuf_tensor("ds_sb", [128, 16], F32))
        scr4 = e(nc.sbuf_tensor("scr4", [128, 2048], F32))
        cb15 = e(nc.sbuf_tensor("cb15", [128, 1], F32))

        ba = big_a.ap()
        bb = big_b.ap()
        ba3 = ba.rearrange("p (c t) -> p c t", c=16)
        bbh = bb.rearrange("p (h t) -> p h t", h=16)

        def xnT(dc, lo, hi):
            return ba[:, dc * 2048 + lo: dc * 2048 + hi]

        def qcols(h, lo, hi):
            return bb[:, h * 2048 + lo: h * 2048 + hi]

        ld_ident = S.dma("sync", qS, ident.ap(), ident_d)
        S.dma("sync", qS, ones_c.ap(), onescol_d)
        S.dma("sync", qS, ones_r.ap(), onesrow_d)
        S.dma("sync", qS, gs_sb.ap(), gs_d)
        S.dma("sync", qS, us_sb.ap(), us_d)
        ld_consts = S.dma("sync", qS, ds_sb.ap(), ds_d)

        tr_evac = {}
        tr_done = {}
        cb15_v = S.op("vector", lambda e_, a=cb15.ap(): e_.memset(a, CBIAS), vE)

        xn_evac = [0] * TC
        with ExitStack() as ps01_stack:
            ep = ps01_stack.enter_context
            qkv_ps = [ep(nc.psum_tensor(f"qkv_ps{i}", [128, 512], F32)) for i in range(2)]
            tr_ps = [ep(nc.psum_tensor(f"tr_ps{i}", [128, 1024], BF16)) for i in range(2)]

            with ExitStack() as p0:
                e0 = p0.enter_context
                xbuf = [e0(nc.sbuf_tensor(f"xbuf{i}", [128, 2048], F32)) for i in range(2)]
                sqbuf = e0(nc.sbuf_tensor("sqbuf", [128, 2048], F32))
                xn_tok = e0(nc.sbuf_tensor("xn_tok", [128, 2048], BF16))
                stat = e0(nc.sbuf_tensor("stat", [128, 224], F32))

                x_ld = [0] * TC
                xn_pass = [0] * TC
                red_done = [0] * TC
                sq_done = [0] * TC
                st = stat.ap()
                last_tr_of_tc = 0
                stat_ms = S.op("vector", lambda e_, a=stat.ap(): e_.memset(a, 0.0), vE)

                for tc in range(TC):
                    xb = xbuf[tc % 2].ap()
                    if tc >= 2:
                        S.wait("sync", vE, xn_pass[tc - 2])
                        S.wait("sync", sE, sq_done[tc - 2])
                    x_ld[tc] = S.dma("sync", qS, xb, x_d[tc * 128:(tc + 1) * 128, :])

                    def c(q, i=tc):
                        return st[:, q * 16 + i: q * 16 + i + 1]

                    S.wait("vector", qS, x_ld[tc])
                    red_done[tc] = S.op("vector", lambda e_, o=c(0), i=xb: e_.reduce_sum(o, i, axis=AX.X), vE)
                    S.wait("scalar", qS, x_ld[tc])
                    if tc == 0:
                        S.wait("scalar", vE, stat_ms)
                    sq_v = sq_done[tc] = S.op("scalar", lambda e_, i=xb, o=sqbuf.ap(), a=c(1): e_.activation(
                        o, i, AF.Square, accum_out=a), sE)
                    S.wait("vector", sE, sq_v)
                    S.wait("vector", vE, red_done[tc])
                    v1 = S.op("vector", lambda e_, o=c(2), i=c(0): e_.tensor_scalar(o, i, -1.0 / 2048.0, None, ALU.mult), vE)
                    v2 = S.op("vector", lambda e_, o=c(3), i=c(1): e_.tensor_scalar(o, i, 1.0 / 2048.0, None, ALU.mult), vE)
                    S.wait("vector", vE, v1)
                    v3 = S.op("vector", lambda e_, o=c(9), i=c(2): e_.tensor_scalar(o, i, i, None, ALU.mult), vE)
                    S.wait("vector", vE, v3)
                    v4 = S.op("vector", lambda e_, o=c(4), i=c(3), j=c(9): e_.tensor_sub(o, i, j), vE)
                    S.wait("vector", vE, v4)
                    v5 = S.op("vector", lambda e_, o=c(10), i=c(4): e_.tensor_scalar(o, i, EPS, None, ALU.add), vE)
                    S.wait("vector", vE, v5)
                    v_rv = S.op("vector", lambda e_, o=c(5), i=c(10): e_.reciprocal(o, i), vE)
                    S.wait("scalar", vE, v_rv)
                    s_rstd = S.op("scalar", lambda e_, o=c(6), i=c(5): e_.activation(o, i, AF.Sqrt), sE)
                    S.wait("vector", sE, s_rstd)
                    v7 = S.op("vector", lambda e_, o=c(8), i=c(6): e_.tensor_scalar(o, i, i, None, ALU.mult), vE)
                    S.wait("vector", vE, v7)
                    v8 = S.op("vector", lambda e_, o=c(11), i=c(8), a=c(10): e_.tensor_scalar(o, i, a, None, ALU.mult), vE)
                    S.wait("vector", vE, v8)
                    v9 = S.op("vector", lambda e_, o=c(12), i=c(11): e_.tensor_scalar(o, i, -0.5, 1.5, ALU.mult, ALU.add), vE)
                    S.wait("vector", vE, v9)
                    v10 = S.op("vector", lambda e_, o=c(7), i=c(6), a=c(12): e_.tensor_scalar(o, i, a, None, ALU.mult), vE)
                    S.wait("vector", vE, v10)
                    S.wait("vector", tE, last_tr_of_tc)
                    xn_pass[tc] = S.op("vector", lambda e_, o=xn_tok.ap(), i=xb, m=c(2), r=c(7):
                                       e_.tensor_scalar(o, i, m, r, ALU.add, ALU.mult), vE)

                    for half in range(2):
                        slot = half
                        S.wait("tensor", vE, xn_pass[tc])
                        if tc == 0 and half == 0:
                            S.wait("tensor", qS, ld_ident)
                        prev = tr_evac.get(slot)
                        if prev:
                            S.wait("tensor", sE, prev)
                        for j in range(8):
                            dc = half * 8 + j
                            v = S.op("tensor", lambda e_, o=tr_ps[slot].ap()[:, j * 128:(j + 1) * 128],
                                     i=xn_tok.ap()[:, dc * 128:(dc + 1) * 128], idn=ident.ap():
                                     e_.transpose(o, i, idn), tE if j == 7 else None)
                        tr_done[slot] = v
                        last_tr_of_tc = v
                        S.wait("scalar", tE, tr_done[slot])
                        dst = ba3[:, half * 8:(half + 1) * 8, tc * 128:(tc + 1) * 128]
                        src = tr_ps[slot].ap().rearrange("p (c t) -> p c t", c=8)
                        xn_evac[tc] = tr_evac[slot] = S.op(
                            "scalar", lambda e_, o=dst, i=src: e_.copy(o, i), sE)

                xn_sp_w = 0
                for dc in range(DCH):
                    S.wait("sync", sE, xn_evac[TC - 1])
                    xn_sp_w = S.dma("sync", qS, xn_sp[dc * 128:(dc + 1) * 128, :], xnT(dc, 0, 2048))
                p0_tE_end = S.cur(tE)
                p0_vE_end = S.cur(vE)
                p0_sE_end = S.cur(sE)

            with ExitStack() as p1:
                e1 = p1.enter_context
                cos_sb = e1(nc.sbuf_tensor("cos_sb", [128, T], F32))
                sin_sb = e1(nc.sbuf_tensor("sin_sb", [128, T], F32))
                wat = [e1(nc.sbuf_tensor(f"wat{i}", [128, 4096], BF16)) for i in range(2)]
                qt4 = e1(nc.sbuf_tensor("qt4", [128, 1024], F32))
                tcos = e1(nc.sbuf_tensor("tcos", [128, 1024], F32))
                tsin = e1(nc.sbuf_tensor("tsin", [128, 1024], BF16))
                roped = e1(nc.sbuf_tensor("roped", [128, 1024], BF16))

                S.wait("sync", vE, p0_vE_end)
                S.wait("sync", sE, p0_sE_end)
                S.wait("sync", tE, p0_tE_end)
                S.dma("sync", qS, cos_sb.ap(), cos_d)
                ld_tab = S.dma("sync", qS, sin_sb.ap(), sin_d)
                S.wait("gpsimd", vE, p0_vE_end)
                S.wait("gpsimd", sE, p0_sE_end)
                S.wait("gpsimd", tE, p0_tE_end)

                wsrc = wattn_d.rearrange("(c p) f -> p c f", p=128)
                wa3 = [w.ap().rearrange("p (c f) -> p c f", c=16) for w in wat]
                wat_ld = [0] * NSTRIP
                wat_fwd = [0] * NSTRIP
                strip_last_mm = [0] * NSTRIP
                ps_free = [None, None]
                rope_v = {}
                evac_last = {}
                gi = 0

                for s_ in range(NSTRIP):
                    slot = s_ % 2
                    if s_ >= 2:
                        S.wait("gpsimd", tE, strip_last_mm[s_ - 2])
                    for hf in range(2):
                        wat_ld[s_] = S.dma("gpsimd", qF, wa3[slot][:, hf * 8:(hf + 1) * 8, :],
                                           wsrc[:, hf * 8:(hf + 1) * 8, s_ * 256:(s_ + 1) * 256])
                    if s_ < 2:
                        wat_fwd[s_] = S.fwd("gpsimd", qF, wat_ld[s_], fG)
                    for tcb in range(4):
                        for tci in range(4):
                            tc = tcb * 4 + tci
                            pslot = gi % 2
                            gi += 1
                            ps = qkv_ps[pslot].ap()[:, 0:256]
                            if s_ < 2:
                                S.wait("tensor", fG, wat_fwd[s_])
                            else:
                                S.wait("tensor", qF, wat_ld[s_])
                            S.wait("tensor", sE, xn_evac[tc])
                            if ps_free[pslot] is not None:
                                S.wait("tensor", sE, ps_free[pslot])
                            for dc in range(DCH):
                                last = dc == DCH - 1
                                v = S.op("tensor", lambda e_, o=ps,
                                         l=xnT(dc, tc * 128, (tc + 1) * 128),
                                         r=wa3[slot][:, dc, :],
                                         st_=(dc == 0), sp_=last:
                                         e_.matmul(o, l, r, start=st_, stop=sp_),
                                         tE if last else None)
                            strip_last_mm[s_] = v
                            S.wait("scalar", tE, v)
                            if tci == 0 and (s_, tcb - 1) in rope_v:
                                S.wait("scalar", vE, rope_v[(s_, tcb - 1)])
                            if tci == 0 and tcb == 0 and (s_ - 1, 3) in rope_v:
                                S.wait("scalar", vE, rope_v[(s_ - 1, 3)])
                            ev = S.op("scalar", lambda e_, o=qt4.ap()[:, tci * 256:(tci + 1) * 256],
                                      i=ps: e_.copy(o, i), sE)
                            ps_free[pslot] = ev
                            evac_last[(s_, tcb)] = ev

                        S.wait("vector", sE, evac_last[(s_, tcb)])
                        if s_ == 0 and tcb == 0:
                            S.wait("vector", qS, ld_tab)
                        cosb = cos_sb.ap()[:, tcb * 512:(tcb + 1) * 512]
                        sinb = sin_sb.ap()[:, tcb * 512:(tcb + 1) * 512]
                        c4 = cosb.rearrange("p (t j) -> p t j", t=4)
                        s4 = sinb.rearrange("p (t j) -> p t j", t=4)
                        if s_ < 8:
                            q4 = qt4.ap().rearrange("p (t h j) -> p t h j", t=4, h=2)
                            t4c = tcos.ap().rearrange("p (t h j) -> p t h j", t=4, h=2)
                            t4s = tsin.ap().rearrange("p (t h j) -> p t h j", t=4, h=2)
                            r4 = roped.ap().rearrange("p (h t j) -> p t h j", h=2, t=4)
                            S.op("vector", lambda e_, o=t4c, a=q4,
                                 b=c4.unsqueeze(2).broadcast_to([128, 4, 2, 128]): e_.tensor_mul(o, a, b))
                            S.op("vector", lambda e_, o=t4s[:, :, :, 0:64], a=q4[:, :, :, 64:128],
                                 b=s4[:, :, 0:64].unsqueeze(2).broadcast_to([128, 4, 2, 64]):
                                 e_.tensor_mul(o, a, b))
                            vm = S.op("vector", lambda e_, o=t4s[:, :, :, 64:128], a=q4[:, :, :, 0:64],
                                      b=s4[:, :, 64:128].unsqueeze(2).broadcast_to([128, 4, 2, 64]):
                                      e_.tensor_mul(o, a, b), vE)
                            S.wait("vector", vE, vm)
                            prev_t = tr_done.get(2)
                            if prev_t:
                                S.wait("vector", tE, prev_t)
                            rope_v[(s_, tcb)] = S.op("vector", lambda e_, o=r4, a=t4c, b=t4s:
                                                     e_.tensor_add(o, a, b), vE)
                            slot_t = tcb % 2
                            S.wait("tensor", vE, rope_v[(s_, tcb)])
                            prev = tr_evac.get(slot_t)
                            if prev:
                                S.wait("tensor", sE, prev)
                            for j in range(8):
                                v = S.op("tensor", lambda e_, o=tr_ps[slot_t].ap()[:, j * 128:(j + 1) * 128],
                                         i=roped.ap()[:, j * 128:(j + 1) * 128], idn=ident.ap():
                                         e_.transpose(o, i, idn), tE if j == 7 else None)
                            tr_done[slot_t] = v
                            tr_done[2] = v
                            S.wait("scalar", tE, v)
                            dstq = bbh[:, s_ * 2:(s_ + 1) * 2, tcb * 512:(tcb + 1) * 512]
                            srcq = tr_ps[slot_t].ap().rearrange("p (h t) -> p h t", h=2)
                            tr_evac[slot_t] = S.op("scalar", lambda e_, o=dstq, i=srcq: e_.copy(o, i), sE)
                        else:
                            k4 = qt4.ap().rearrange("p (t c j) -> p t c j", t=4, c=2)[:, :, 0, :]
                            t4ck = tcos.ap()[:, 0:512].rearrange("p (t j) -> p t j", t=4)
                            t4sk = tsin.ap()[:, 0:512].rearrange("p (t j) -> p t j", t=4)
                            r4k = roped.ap()[:, 0:512].rearrange("p (t j) -> p t j", t=4)
                            S.op("vector", lambda e_, o=t4ck, a=k4, b=c4: e_.tensor_mul(o, a, b))
                            S.op("vector", lambda e_, o=t4sk[:, :, 0:64], a=k4[:, :, 64:128],
                                 b=s4[:, :, 0:64]: e_.tensor_mul(o, a, b))
                            vm = S.op("vector", lambda e_, o=t4sk[:, :, 64:128], a=k4[:, :, 0:64],
                                      b=s4[:, :, 64:128]: e_.tensor_mul(o, a, b), vE)
                            S.wait("vector", vE, vm)
                            prev_t = tr_done.get(2)
                            if prev_t:
                                S.wait("vector", tE, prev_t)
                            rope_v[(s_, tcb)] = S.op("vector", lambda e_, o=r4k, a=t4ck, b=t4sk:
                                                     e_.tensor_add(o, a, b), vE)
                            slot_t = tcb % 2
                            S.wait("tensor", vE, rope_v[(s_, tcb)])
                            prev = tr_evac.get(slot_t)
                            if prev:
                                S.wait("tensor", sE, prev)
                            for j in range(4):
                                v = S.op("tensor", lambda e_, o=tr_ps[slot_t].ap()[:, j * 128:(j + 1) * 128],
                                         i=roped.ap()[:, j * 128:(j + 1) * 128], idn=ident.ap():
                                         e_.transpose(o, i, idn), tE if j == 3 else None)
                            tr_done[slot_t] = v
                            tr_done[2] = v
                            S.wait("scalar", tE, v)
                            tr_evac[slot_t] = S.op(
                                "scalar", lambda e_, o=k_sb.ap()[:, tcb * 512:(tcb + 1) * 512],
                                i=tr_ps[slot_t].ap()[:, 0:512]: e_.copy(o, i), sE)
                            for tci in range(4):
                                tc = tcb * 4 + tci
                                S.op("scalar", lambda e_, o=v_sb.ap()[:, tc * 128:(tc + 1) * 128],
                                     i=qt4.ap()[:, tci * 256 + 128:(tci + 1) * 256]: e_.copy(o, i),
                                     sE if tci == 3 else None)

                p1_tE_end = S.cur(tE)
                p1_sE_end = S.cur(sE)
                p1_vE_end = S.cur(vE)

        S.wait("gpsimd", tE, p1_tE_end)
        wout_ld = [0] * DCH
        wo_src = wout_d.rearrange("(c p) f -> p c f", p=128)
        for fc in range(DCH):
            wout_ld[fc] = S.dma("gpsimd", qG, ba[:, fc * 2048:(fc + 1) * 2048], wo_src[:, fc, :])
        wout_f = S.fwd("gpsimd", qG, wout_ld[DCH - 1], fG)
        wg_cast = [0] * 32
        wu_cast = [0] * 32
        for sp in range(32):
            wg_cast[sp] = S.dma("gpsimd", qG, wg_b_d[sp], wg_i_d[sp])
        for sp in range(32):
            wu_cast[sp] = S.dma("gpsimd", qG, wu_b_d[sp], wu_i_d[sp])
        gu_cast_f = S.fwd("gpsimd", qG, wu_cast[31], fG)

        dbg_qkv_dump = 0
        if debug is True:
            S.wait("sync", vE, p1_vE_end)
            S.wait("sync", sE, p1_sE_end)
            S.dma("sync", qS, dbg_q, bb)
            S.dma("sync", qS, dbg_k, k_sb.ap())
            dbg_qkv_dump = S.dma("sync", qS, dbg_v, v_sb.ap())

        with ExitStack() as p2:
            e2 = p2.enter_context
            attn_sb = e2(nc.sbuf_tensor("attn_sb", [128, 4096], BF16))
            r_sb = e2(nc.sbuf_tensor("r_sb", [128, 1024], F32))
            den_sb = e2(nc.sbuf_tensor("den_sb", [1, 1024], F32))
            prep = [e2(nc.sbuf_tensor(f"prep{i}", [128, 4096], BF16)) for i in range(2)]
            s_stage = [e2(nc.sbuf_tensor(f"s_stage{i}", [128, 2048], BF16)) for i in range(4)]

            wd_ts = [0] * 32
            wd_w = [0] * 32
            wd_w_f = [0] * 32
            for sp in range(32):
                slot = sp % 2
                if sp >= 2:
                    S.wait("gpsimd", fS, wd_w_f[sp - 2])
                cast_v = S.dma("gpsimd", qG, prep[slot].ap(), wd_i_d[sp])
                S.wait("gpsimd", qG, cast_v)
                wd_ts[sp] = S.op("gpsimd", lambda e_, a=prep[slot].ap():
                                 e_.tensor_scalar(a, a, -128.0, None, ALU.add), fG)
                S.wait("sync", fG, wd_ts[sp])
                wd_w[sp] = S.dma("sync", qS, wd_b_d[sp], prep[slot].ap())
                wd_w_f[sp] = S.fwd("sync", qS, wd_w[sp], fS)

            with ExitStack() as ps2:
                ep2 = ps2.enter_context
                sc_ps = [ep2(nc.psum_tensor(f"sc_ps{i}", [128, 1024], F32)) for i in range(2)]
                av_ps = ep2(nc.psum_tensor("av_ps", [128, 1024], F32))
                den_ps = ep2(nc.psum_tensor("den_ps", [128, 1024], F32))

                NKC = 16
                exp_inc = {}
                den_done = {}
                recip_inc = {}
                r_evac_inc = {}
                avtt_inc = {}

                for pi in range(32):
                    h, qh = pi // 2, pi % 2
                    qlo = qh * 1024

                    def emit_avden(kc):
                        aslot = kc % 4
                        S.wait("tensor", sE, exp_inc[(pi, kc)])
                        if kc == 0 and pi > 0:
                            S.wait("tensor", vE, avtt_inc[pi - 1])
                            S.wait("tensor", sE, r_evac_inc[pi - 1])
                        for half in range(2):
                            S.op("tensor", lambda e_, o=av_ps.ap()[:, half * 512:(half + 1) * 512],
                                 l=v_sb.ap()[:, kc * 128:(kc + 1) * 128],
                                 r=attn_sb.ap()[:, aslot * 1024 + half * 512: aslot * 1024 + (half + 1) * 512],
                                 st_=(kc == 0), sp_=(kc == NKC - 1):
                                 e_.matmul(o, l, r, start=st_, stop=sp_))
                        for half in range(2):
                            v = S.op("tensor", lambda e_, o=den_ps.ap()[0:1, half * 512:(half + 1) * 512],
                                     l=ones_c.ap(),
                                     r=attn_sb.ap()[:, aslot * 1024 + half * 512: aslot * 1024 + (half + 1) * 512],
                                     st_=(kc == 0), sp_=(kc == NKC - 1):
                                     e_.matmul(o, l, r, start=st_, stop=sp_),
                                     tE if half == 1 else None)
                        den_done[(pi, kc)] = v

                    for kc in range(NKC):
                        pslot = kc % 2
                        if kc >= 2:
                            S.wait("tensor", sE, exp_inc[(pi, kc - 2)])
                        elif kc == 0 and pi > 0:
                            S.wait("tensor", sE, exp_inc[(pi - 1, 14)])
                        elif kc == 1 and pi > 0:
                            S.wait("tensor", sE, exp_inc[(pi - 1, 15)])
                        if pi == 0 and kc == 0:
                            S.wait("tensor", sE, p1_sE_end)
                            S.wait("tensor", vE, p1_vE_end)
                            S.wait("tensor", qS, dbg_qkv_dump)
                            S.wait("scalar", vE, cb15_v)
                        for half in range(2):
                            v = S.op("tensor", lambda e_, o=sc_ps[pslot].ap()[:, half * 512:(half + 1) * 512],
                                     l=k_sb.ap()[:, kc * 128:(kc + 1) * 128],
                                     r=qcols(h, qlo + half * 512, qlo + (half + 1) * 512):
                                     e_.matmul(o, l, r, start=True, stop=True),
                                     tE if half == 1 else None)
                        sc_done = v
                        aslot = kc % 4
                        S.wait("scalar", tE, sc_done)
                        if kc < 4:
                            if pi > 0:
                                S.wait("scalar", tE, den_done[(pi - 1, 12 + kc)])
                        else:
                            S.wait("scalar", tE, den_done[(pi, kc - 4)])
                        exp_inc[(pi, kc)] = S.op(
                            "scalar", lambda e_, o=attn_sb.ap()[:, aslot * 1024:(aslot + 1) * 1024],
                            i=sc_ps[pslot].ap(), b_=cb15.ap(): e_.activation(o, i, AF.Exp, bias=b_, scale=INVSQ), sE)
                        if kc >= 2:
                            emit_avden(kc - 2)
                    emit_avden(NKC - 2)
                    emit_avden(NKC - 1)
                    S.wait("vector", tE, den_done[(pi, 15)])
                    recip_inc[pi] = S.op("vector", lambda e_, o=den_sb.ap(), i=den_ps.ap()[0:1, :]:
                                         e_.reciprocal(o, i), vE)
                    S.wait("tensor", vE, recip_inc[pi])
                    for half in range(2):
                        v = S.op("tensor", lambda e_, o=den_ps.ap()[:, half * 512:(half + 1) * 512],
                                 l=ones_r.ap(), r=den_sb.ap()[:, half * 512:(half + 1) * 512]:
                                 e_.matmul(o, l, r, start=True, stop=True),
                                 tE if half == 1 else None)
                    r_done = v
                    S.wait("scalar", tE, r_done)
                    r_evac_inc[pi] = S.op("scalar", lambda e_, o=r_sb.ap(), i=den_ps.ap():
                                          e_.copy(o, i), sE)
                    S.wait("vector", sE, r_evac_inc[pi])
                    avtt_inc[pi] = S.op("vector", lambda e_, o=qcols(h, qlo, qlo + 1024),
                                        a=av_ps.ap(), b=r_sb.ap(): e_.tensor_mul(o, a, b), vE)

                p2_vE_end = S.cur(vE)

            dbg_av_dump = 0
            if debug is True:
                S.wait("sync", vE, p2_vE_end)
                dbg_av_dump = S.dma("sync", qS, dbg_av, bb)
            with ExitStack() as ps3:
                ep3 = ps3.enter_context
                po_ps = [ep3(nc.psum_tensor(f"po_ps{i}", [128, 512], F32)) for i in range(2)]

                po_evac = {}
                s_store = [0] * DCH
                s_store_f = [0] * DCH
                g3 = 0
                for ec in range(DCH):
                    for st4 in range(4):
                        pslot = g3 % 2
                        if g3 >= 2:
                            S.wait("tensor", sE, po_evac[g3 - 2])
                        if g3 == 0:
                            S.wait("tensor", vE, p2_vE_end)
                            S.wait("tensor", fG, wout_f)
                        for fc in range(DCH):
                            last = fc == DCH - 1
                            v = S.op("tensor", lambda e_, o=po_ps[pslot].ap(),
                                     l=ba[:, fc * 2048 + ec * 128: fc * 2048 + (ec + 1) * 128],
                                     r=qcols(fc, st4 * 512, (st4 + 1) * 512),
                                     st_=(fc == 0), sp_=last:
                                     e_.matmul(o, l, r, start=st_, stop=sp_), tE if last else None)
                        S.wait("scalar", tE, v)
                        if ec >= 4 and st4 == 0:
                            S.wait("scalar", fG, s_store_f[ec - 4])
                        po_evac[g3] = S.op(
                            "scalar", lambda e_, o=s_stage[ec % 4].ap()[:, st4 * 512:(st4 + 1) * 512],
                            i=po_ps[pslot].ap(): e_.copy(o, i), sE)
                        g3 += 1
                    S.wait("gpsimd", sE, po_evac[g3 - 1])
                    s_store[ec] = S.dma("gpsimd", qG, s_sp[ec * 128:(ec + 1) * 128, :],
                                        s_stage[ec % 4].ap())
                    s_store_f[ec] = S.fwd("gpsimd", qG, s_store[ec], fG)
                p3_tE_end = S.cur(tE)
                p3_sE_end = S.cur(sE)
                p3_qG_s_last = s_store[DCH - 1]

        with ExitStack() as p4:
            e4 = p4.enter_context
            x_tile = [e4(nc.sbuf_tensor(f"x_tile{i}", [128, 1024], F32)) for i in range(2)]
            out_sb = [e4(nc.sbuf_tensor(f"out_sb{i}", [128, 2048], F32)) for i in range(2)]
            comb_sb = e4(nc.sbuf_tensor("comb_sb", [128, 8192], BF16))
            wd_buf = [e4(nc.sbuf_tensor(f"wd_buf{i}", [128, 4096], BF16)) for i in range(2)]

            with ExitStack() as ps4:
                ep4 = ps4.enter_context
                g_ps = [ep4(nc.psum_tensor(f"g_ps{i}", [128, 512], F32)) for i in range(2)]
                u_ps = [ep4(nc.psum_tensor(f"u_ps{i}", [128, 512], F32)) for i in range(2)]
                d_ps = [ep4(nc.psum_tensor(f"d_ps{i}", [128, 512], F32)) for i in range(2)]
                tr4 = [ep4(nc.psum_tensor(f"tr4_{i}", [128, 1024], BF16)) for i in range(2)]

                wg_buf = [ba[:, 0:4096], ba[:, 4096:8192]]
                wu_buf = [ba[:, 8192:12288], ba[:, 12288:16384]]
                xn_blk = ba[:, 16384:24576]
                s_blk = ba[:, 24576:32768]
                hmid = bb

                xnsp3 = xn_sp.rearrange("(c p) t -> p c t", p=128)
                ssp3 = s_sp.rearrange("(c p) t -> p c t", p=128)
                xnb3 = xn_blk.rearrange("p (c t) -> p c t", c=16)
                sb3 = s_blk.rearrange("p (c t) -> p c t", c=16)
                cb3 = comb_sb.ap().rearrange("p (c t) -> p c t", c=16)

                dbg3_gates = {}
                gu_mm_done = [0] * NBLK
                tp_done = [0] * NBLK
                stt_done = {}
                comb_done = {}
                silu_done = {}
                dsilu_done = {}
                dn_grp = {}
                tt_add = {}
                out_store = {}
                gu_group = {}

                dbg_hm_dump = 0
                for b in range(NBLK):
                    if debug is True and b == 1:
                        S.wait("sync", vE, stt_done[(0, 63)])
                        S.wait("sync", tE, dn_grp[(0, 31)])
                        dbg_hm_dump = S.dma("sync", qS, dbg_hm, bb)
                    if b >= 1:
                        S.wait("sync", tE, gu_mm_done[b - 1])
                    else:
                        S.wait("sync", tE, p3_tE_end)
                    S.wait("sync", qS, xn_sp_w)
                    xn_ld = 0
                    for hf in range(2):
                        xn_ld = S.dma("sync", qD, xnb3[:, hf * 8:(hf + 1) * 8, :],
                                      xnsp3[:, hf * 8:(hf + 1) * 8, b * 512:(b + 1) * 512])
                    xn_fwd = S.fwd("sync", qD, xn_ld, fS)
                    if b >= 1:
                        S.wait("gpsimd", vE, comb_done[(b - 1, 7)])
                        if debug == 3 and 1 in dbg3_gates:
                            S.wait("gpsimd", qS, dbg3_gates[1])
                    else:
                        S.wait("gpsimd", tE, p3_tE_end)
                    S.wait("gpsimd", qG, s_store[DCH - 1])
                    s_ld = 0
                    for hf in range(2):
                        s_ld = S.dma("gpsimd", qH, sb3[:, hf * 8:(hf + 1) * 8, :],
                                     ssp3[:, hf * 8:(hf + 1) * 8, b * 512:(b + 1) * 512])
                    s_fwd = S.fwd("gpsimd", qH, s_ld, fG)

                    for sp in range(32):
                        slot = sp % 2
                        if sp >= 2:
                            S.wait("sync", tE, gu_group[(b, sp * 2 - 3)])
                        elif b >= 1:
                            S.wait("sync", tE, gu_mm_done[b - 1])
                            S.wait("sync", tE, tp_done[b - 1])
                            if debug == 3 and 1 in dbg3_gates:
                                S.wait("sync", qS, dbg3_gates[1])
                        else:
                            S.wait("sync", tE, p3_tE_end)
                        S.wait("sync", fG, gu_cast_f)
                        wg_ld = S.dma("sync", qA, wg_buf[slot], wg_b_d[sp])
                        if sp < 2:
                            wg_fwd = S.fwd("sync", qA, wg_ld, fS)
                        S.wait("sync", fG, gu_cast_f)
                        wu_ld = S.dma("sync", qB, wu_buf[slot], wu_b_d[sp])
                        if sp < 2:
                            wu_fwd = S.fwd("sync", qB, wu_ld, fS)
                        for ici in range(2):
                            ic = sp * 2 + ici
                            pslot = ic % 2
                            if sp < 2:
                                S.wait("tensor", fS, wg_fwd)
                            else:
                                S.wait("tensor", qA, wg_ld)
                            if b == 0 and ic == 0:
                                S.wait("tensor", fS, xn_fwd)
                                S.wait("tensor", qS, ld_consts)
                                S.wait("tensor", sE, p3_sE_end)
                            if ic >= 2:
                                S.wait("tensor", sE, silu_done[(b, ic - 2)])
                            elif b >= 1:
                                S.wait("tensor", sE, silu_done[(b - 1, 62 + ic)])
                            for hc in range(DCH):
                                v = S.op("tensor", lambda e_, o=g_ps[pslot].ap(),
                                         l=wg_buf[slot][:, (ici * 16 + hc) * 128:(ici * 16 + hc + 1) * 128],
                                         r=xnb3[:, hc, :], st_=(hc == 0), sp_=(hc == DCH - 1):
                                         e_.matmul(o, l, r, start=st_, stop=sp_),
                                         tE if hc == DCH - 1 else None)
                            g_done = v
                            if sp < 2:
                                S.wait("tensor", fS, wu_fwd)
                            else:
                                S.wait("tensor", qB, wu_ld)
                            if ic >= 2:
                                S.wait("tensor", vE, stt_done[(b, ic - 2)])
                            elif b >= 1:
                                S.wait("tensor", vE, stt_done[(b - 1, 62 + ic)])
                            if debug and b == 1 and ic == 0:
                                S.wait("vector", qS, dbg_hm_dump)
                            for hc in range(DCH):
                                v = S.op("tensor", lambda e_, o=u_ps[pslot].ap(),
                                         l=wu_buf[slot][:, (ici * 16 + hc) * 128:(ici * 16 + hc + 1) * 128],
                                         r=xnb3[:, hc, :], st_=(hc == 0), sp_=(hc == DCH - 1):
                                         e_.matmul(o, l, r, start=st_, stop=sp_),
                                         tE if hc == DCH - 1 else None)
                            u_done = v
                            gu_group[(b, ic)] = u_done
                            S.wait("scalar", tE, g_done)
                            if ic >= 2:
                                S.wait("scalar", vE, stt_done[(b, ic - 2)])
                            elif b >= 1:
                                S.wait("scalar", vE, stt_done[(b - 1, 62 + ic)])
                            silu_done[(b, ic)] = S.op(
                                "scalar", lambda e_, o=scr4.ap()[:, pslot * 512:(pslot + 1) * 512],
                                i=g_ps[pslot].ap(), sc=gs_sb.ap()[:, ic:ic + 1]:
                                e_.activation(o, i, AF.Silu, scale=sc), sE)
                            S.wait("vector", tE, u_done)
                            S.wait("vector", sE, silu_done[(b, ic)])
                            if debug is True and b == 0 and ic == 0:
                                S.wait("vector", qS, dbg_av_dump)
                            stt_done[(b, ic)] = S.op(
                                "vector", lambda e_, o=hmid[:, ic * 512:(ic + 1) * 512],
                                i0=u_ps[pslot].ap(), sc=us_sb.ap()[:, ic:ic + 1],
                                i1=scr4.ap()[:, pslot * 512:(pslot + 1) * 512]:
                                e_.scalar_tensor_tensor(o, i0, sc, i1, ALU.mult, ALU.mult), vE)
                    gu_mm_done[b] = gu_group[(b, 63)]
                    if debug is True and b == 1:
                        pass

                    for sw in range(8):
                        for sp4 in range(4):
                            gsp = sw * 4 + sp4
                            dslot = gsp % 2
                            if gsp >= 2:
                                S.wait("sync", tE, dn_grp[(b, gsp - 2)])
                            elif b >= 1:
                                S.wait("sync", tE, dn_grp[(b - 1, 30 + gsp)])
                            else:
                                S.wait("sync", tE, p3_tE_end)
                                S.wait("sync", sE, p3_sE_end)
                                S.wait("sync", qS, wd_w[31])
                                S.wait("sync", qG, p3_qG_s_last)
                            S.wait("sync", qS, wd_w[gsp])
                            wd_ld = S.dma("sync", qC, wd_buf[dslot].ap(), wd_b_d[gsp])
                            wd_fwd = S.fwd("sync", qC, wd_ld, fS)
                            S.wait("tensor", fS, wd_fwd)
                            for ici in range(16):
                                ic = sp4 * 16 + ici
                                S.wait("tensor", vE, stt_done[(b, 63)] if sw == 0 else stt_done[(b, ic)])
                                if ic == 0 and sw > 0:
                                    S.wait("tensor", sE, dsilu_done[(b, sw - 1)])
                                elif ic == 0 and b >= 1:
                                    S.wait("tensor", sE, dsilu_done[(b - 1, 7)])
                                for hcs in range(2):
                                    v = S.op("tensor", lambda e_, o=d_ps[hcs].ap(),
                                             l=wd_buf[dslot].ap()[:, (ici * 2 + hcs) * 128:(ici * 2 + hcs + 1) * 128],
                                             r=hmid[:, ic * 512:(ic + 1) * 512],
                                             st_=(ic == 0), sp_=(ic == NIC - 1):
                                             e_.matmul(o, l, r, start=st_, stop=sp_),
                                             tE if hcs == 1 else None)
                            dn_grp[(b, gsp)] = v
                        if debug == 3 and sw == 1:
                            S.wait("scalar", qS, dbg_ga_dump)
                        for hcs in range(2):
                            hc = sw * 2 + hcs
                            S.wait("scalar", tE, dn_grp[(b, sw * 4 + 3)])
                            if sw >= 1:
                                S.wait("scalar", vE, comb_done[(b, sw - 1)])
                            elif b >= 1:
                                S.wait("scalar", vE, comb_done[(b - 1, 7)])
                            sv = S.op("scalar", lambda e_, o=scr4.ap()[:, 1024 + hcs * 512: 1536 + hcs * 512],
                                      i=d_ps[hcs].ap(), sc=ds_sb.ap()[:, hc:hc + 1]:
                                      e_.activation(o, i, AF.Silu, scale=sc), sE)
                            if hcs == 1:
                                dsilu_done[(b, sw)] = sv
                            S.wait("vector", sE, sv)
                            if sw == 0 and hcs == 0:
                                S.wait("vector", fG, s_fwd)
                            comb_done[(b, sw)] = S.op(
                                "vector", lambda e_, o=cb3[:, hc, :], a=sb3[:, hc, :],
                                b_=scr4.ap()[:, 1024 + hcs * 512: 1536 + hcs * 512]:
                                e_.tensor_mul(o, a, b_), vE)
                            if debug == 3 and sw == 0:
                                S.wait("sync", sE, sv)
                                dbg_ga_dump = S.dma("sync", qS, dbg_ga[b][:, hcs * 512:(hcs + 1) * 512],
                                                    scr4.ap()[:, 1024 + hcs * 512: 1536 + hcs * 512])

                    for tsub in range(4):
                        trow = b * 512 + tsub * 128
                        oslot = (b * 4 + tsub) % 2
                        for r in range(2):
                            key_prev = (b, tsub - 1, r) if tsub >= 1 else ((b - 1, 3, r) if b >= 1 else None)
                            if key_prev in tt_add:
                                S.wait("gpsimd", vE, tt_add[key_prev])
                            xt_ld = S.dma("gpsimd", qE, x_tile[r].ap(),
                                          x_d[trow:trow + 128, r * 1024:(r + 1) * 1024])
                            S.wait("tensor", vE, comb_done[(b, 7)])
                            if key_prev in tt_add:
                                S.wait("tensor", vE, tt_add[key_prev])
                            for j in range(8):
                                hc = r * 8 + j
                                v = S.op("tensor", lambda e_, o=tr4[r].ap()[:, j * 128:(j + 1) * 128],
                                         i=cb3[:, hc, tsub * 128:(tsub + 1) * 128], idn=ident.ap():
                                         e_.transpose(o, i, idn), tE if j == 7 else None)
                            tp_done[b] = v
                            S.wait("vector", tE, v)
                            S.wait("vector", qE, xt_ld)
                            pkey = (b * 4 + tsub) - 2
                            if pkey >= 0:
                                S.wait("vector", qO, out_store[pkey])
                            tt_add[(b, tsub, r)] = S.op(
                                "vector", lambda e_, o=out_sb[oslot].ap()[:, r * 1024:(r + 1) * 1024],
                                a=tr4[r].ap(), b_=x_tile[r].ap():
                                e_.tensor_add(o, a, b_), vE)
                        S.wait("gpsimd", vE, tt_add[(b, tsub, 1)])
                        out_store[b * 4 + tsub] = S.dma("gpsimd", qO, out_d[trow:trow + 128, :],
                                                        out_sb[oslot].ap())
                    if debug == 3:
                        S.wait("sync", tE, tp_done[b])
                        S.dma("sync", qS, dbg_cb[b], ba[:, 0:8192])
                        dbg_cb_dump = S.dma("sync", qS, dbg_sb[b], ba[:, 24576:32768])
                        if b == 0:
                            S.wait("sync", vE, stt_done[(0, 63)])
                            S.wait("sync", tE, dn_grp[(0, 31)])
                            dbg_hm_d3 = S.dma("sync", qS, dbg_hm, bb)
                            dbg3_gates[0] = dbg_hm_d3
                        dbg3_gates[1] = dbg_cb_dump

                S.wait("gpsimd", qE, S.cur(qE))
                S.wait("gpsimd", qH, S.cur(qH))
                S.wait("gpsimd", qO, S.cur(qO))
                S.wait("gpsimd", qG, S.cur(qG))
                S.wait("sync", qS, S.cur(qS))
                S.wait("sync", qA, S.cur(qA))
                S.wait("sync", qB, S.cur(qB))
                S.wait("sync", qC, S.cur(qC))
                S.wait("sync", qD, S.cur(qD))

        with nc.Block() as block:
            @block.sync
            def _(eng):
                for fn in S.ops["sync"]:
                    fn(eng)

            @block.gpsimd
            def _(eng):
                for fn in S.ops["gpsimd"]:
                    fn(eng)

            @block.tensor
            def _(eng):
                for fn in S.ops["tensor"]:
                    fn(eng)

            @block.vector
            def _(eng):
                for fn in S.ops["vector"]:
                    fn(eng)

            @block.scalar
            def _(eng):
                for fn in S.ops["scalar"]:
                    fn(eng)

    return nc


def _host_prep(L0_attn_x, L0_attn_w_attn, L0_attn_w_out, L0_ffn_gate_w, L0_ffn_gate_scale,
               L0_ffn_up_w, L0_ffn_up_scale, L0_ffn_down_w, L0_ffn_down_scale):
    def gu_img(w):
        a = np.ascontiguousarray(w.T)
        a = a.reshape(16, 128, 64, 128)
        a = a.transpose(2, 1, 0, 3)
        a = a.reshape(32, 2, 128, 16, 128).transpose(0, 2, 1, 3, 4)
        return np.ascontiguousarray(a.reshape(32, 128, 4096))

    def d_img(w):
        a = np.ascontiguousarray(w.T)
        a = a.reshape(4, 16, 128, 8, 2, 128)
        a = a.transpose(3, 0, 2, 1, 4, 5)
        return np.ascontiguousarray(a.reshape(32, 128, 4096))

    wg_img = gu_img(np.asarray(L0_ffn_gate_w))
    wu_img = gu_img(np.asarray(L0_ffn_up_w))
    wd_img = d_img(np.asarray(L0_ffn_down_w))

    gs = np.ascontiguousarray(np.asarray(L0_ffn_gate_scale, dtype=np.float32).reshape(64, 128).T)
    us = np.ascontiguousarray(np.asarray(L0_ffn_up_scale, dtype=np.float32).reshape(64, 128).T)
    ds = np.ascontiguousarray(np.asarray(L0_ffn_down_scale, dtype=np.float32).reshape(16, 128).T)

    jj = np.arange(0, 128, 2, dtype=np.float32)
    pos = np.float32(10000.0) ** ((np.float32(-2.0) * jj - np.float32(1.0)) / np.float32(128.0))
    tt = np.arange(T, dtype=np.float32)
    ang = (tt[:, None] * pos[None, :]).astype(np.float32)
    cos = np.cos(np.concatenate([ang, ang], axis=1)).astype(np.float32)
    sin = np.sin(np.concatenate([ang, ang], axis=1)).astype(np.float32)
    sin_s = np.concatenate([-sin[:, :64], sin[:, 64:]], axis=1)
    cos_t = np.ascontiguousarray(cos.reshape(16, 128, 128).transpose(1, 0, 2).reshape(128, T))
    sin_t = np.ascontiguousarray(sin_s.reshape(16, 128, 128).transpose(1, 0, 2).reshape(128, T))

    shared = dict(
        w_attn=np.ascontiguousarray(np.asarray(L0_attn_w_attn, dtype=np.float32)),
        w_out=np.ascontiguousarray(np.asarray(L0_attn_w_out, dtype=np.float32)),
        wg_img_i=wg_img, wu_img_i=wu_img, wd_img_i=wd_img,
        gscale=gs, uscale=us, dscale=ds,
        cos_t=cos_t, sin_t=sin_t,
        ident=np.eye(128, dtype=ml_dtypes.bfloat16),
        ones_col=np.ones((128, 1), dtype=ml_dtypes.bfloat16),
        ones_row=np.ones((1, 128), dtype=np.float32),
    )
    in_maps = []
    xa = np.asarray(L0_attn_x, dtype=np.float32)
    for c in range(8):
        m = dict(shared)
        m["x"] = np.ascontiguousarray(xa[c])
        in_maps.append(m)
    return in_maps


def kernel(**inputs):
    in_maps = _host_prep(**inputs)
    nc = build_program(debug=False)
    res = run_bass_kernel_spmd(nc, in_maps, core_ids=list(range(8)))
    return np.stack([np.asarray(r["out"], dtype=np.float32) for r in res.results], axis=0)


# version 14
# speedup vs baseline: 1.0509x; 1.0509x over previous
import numpy as np
import ml_dtypes
import concourse.bass as bass
import concourse.mybir as mybir
from concourse.bass_utils import run_bass_kernel_spmd
from contextlib import ExitStack

F32 = mybir.dt.float32
BF16 = mybir.dt.bfloat16
I32 = mybir.dt.int32
AF = mybir.ActivationFunctionType
ALU = mybir.AluOpType
AX = mybir.AxisListType

T = 2048
D = 2048
NH = 16
HD = 128
II = 8192
TC = 16
DCH = 16
NIC = 64
BLK = 512
NBLK = 4
NSTRIP = 9
INVSQ = float(np.float32(1.0) / np.sqrt(np.float32(128.0)))
CBIAS = -15.0
EPS = 1e-5

ENGS = ("sync", "gpsimd", "tensor", "vector", "scalar")


class Sched:
    def __init__(self):
        self.ops = {k: [] for k in ENGS}
        self.n = {}

    def cur(self, sem):
        return self.n.get(sem.name, 0)

    def _bump(self, sem, by):
        v = self.n.get(sem.name, 0) + by
        self.n[sem.name] = v
        return v

    def emit(self, eng, fn):
        self.ops[eng].append(fn)

    def wait(self, eng, sem, val):
        if val is None or val <= 0:
            return
        self.emit(eng, lambda e, s=sem, v=val: e.wait_ge(s, v))

    def dma(self, eng, sem, dst, src):
        v = self._bump(sem, 16)
        self.emit(eng, lambda e, d=dst, s=src, sm=sem: e.dma_start(d, s).then_inc(sm, 16))
        return v

    def op(self, eng, build, sem=None):
        if sem is None:
            self.emit(eng, lambda e, b=build: b(e))
            return None
        v = self._bump(sem, 1)
        self.emit(eng, lambda e, b=build, sm=sem: b(e).then_inc(sm, 1))
        return v

    def fwd(self, eng, qsem, qval, fsem):
        self.wait(eng, qsem, qval)
        v = self._bump(fsem, 1)
        self.emit(eng, lambda e, s=fsem: e.sem_inc(s, 1))
        return v


def build_program(debug=False):
    nc = bass.Bass("TRN2", target_bir_lowering=False)

    x_d = nc.dram_tensor("x", [T, D], F32, kind="ExternalInput").ap()
    wattn_d = nc.dram_tensor("w_attn", [D, 2304], F32, kind="ExternalInput").ap()
    wout_d = nc.dram_tensor("w_out", [D, D], F32, kind="ExternalInput").ap()
    wg_i_d = nc.dram_tensor("wg_img_i", [32, 128, 4096], I32, kind="ExternalInput").ap()
    wu_i_d = nc.dram_tensor("wu_img_i", [32, 128, 4096], I32, kind="ExternalInput").ap()
    wd_i_d = nc.dram_tensor("wd_img_i", [32, 128, 4096], I32, kind="ExternalInput").ap()
    gs_d = nc.dram_tensor("gscale", [128, 64], F32, kind="ExternalInput").ap()
    us_d = nc.dram_tensor("uscale", [128, 64], F32, kind="ExternalInput").ap()
    ds_d = nc.dram_tensor("dscale", [128, 16], F32, kind="ExternalInput").ap()
    cos_d = nc.dram_tensor("cos_t", [128, T], F32, kind="ExternalInput").ap()
    sin_d = nc.dram_tensor("sin_t", [128, T], F32, kind="ExternalInput").ap()
    ident_d = nc.dram_tensor("ident", [128, 128], BF16, kind="ExternalInput").ap()
    onescol_d = nc.dram_tensor("ones_col", [128, 1], BF16, kind="ExternalInput").ap()
    onesrow_d = nc.dram_tensor("ones_row", [1, 128], F32, kind="ExternalInput").ap()

    out_d = nc.dram_tensor("out", [T, D], F32, kind="ExternalOutput").ap()

    iknd = "ExternalOutput" if debug else "Internal"
    xn_sp = nc.dram_tensor("xn_sp", [D, T], BF16, kind=iknd).ap()
    s_sp = nc.dram_tensor("s_sp", [D, T], BF16, kind=iknd).ap()
    wg_b_d = nc.dram_tensor("wg_img_b", [32, 128, 4096], BF16, kind=iknd).ap()
    wu_b_d = nc.dram_tensor("wu_img_b", [32, 128, 4096], BF16, kind=iknd).ap()
    wd_b_d = nc.dram_tensor("wd_img_b", [32, 128, 4096], BF16, kind=iknd).ap()
    if debug == 3:
        dbg_ga = nc.dram_tensor("dbg_ga", [4, 128, 1024], F32, kind="ExternalOutput").ap()
        dbg_cb = nc.dram_tensor("dbg_cb", [4, 128, 8192], BF16, kind="ExternalOutput").ap()
        dbg_hm = nc.dram_tensor("dbg_hm", [128, 32768], BF16, kind="ExternalOutput").ap()
        dbg_sb = nc.dram_tensor("dbg_sb", [4, 128, 8192], BF16, kind="ExternalOutput").ap()
    if debug is True:
        dbg_q = nc.dram_tensor("dbg_q", [128, 32768], BF16, kind="ExternalOutput").ap()
        dbg_k = nc.dram_tensor("dbg_k", [128, T], BF16, kind="ExternalOutput").ap()
        dbg_v = nc.dram_tensor("dbg_v", [128, T], BF16, kind="ExternalOutput").ap()
        dbg_av = nc.dram_tensor("dbg_av", [128, 32768], BF16, kind="ExternalOutput").ap()
        dbg_hm = nc.dram_tensor("dbg_hm", [128, 32768], BF16, kind="ExternalOutput").ap()

    S = Sched()

    with ExitStack() as top:
        e = top.enter_context
        qS = e(nc.semaphore("qS"))
        qG = e(nc.semaphore("qG"))
        qA = e(nc.semaphore("qA"))
        qB = e(nc.semaphore("qB"))
        qC = e(nc.semaphore("qC"))
        qD = e(nc.semaphore("qD"))
        qE = e(nc.semaphore("qE"))
        qF = e(nc.semaphore("qF"))
        qH = e(nc.semaphore("qH"))
        qO = e(nc.semaphore("qO"))
        fS = e(nc.semaphore("fS"))
        fG = e(nc.semaphore("fG"))
        tE = e(nc.semaphore("tE"))
        vE = e(nc.semaphore("vE"))
        sE = e(nc.semaphore("sE"))

        big_a = e(nc.sbuf_tensor("big_a", [128, 32768], BF16))
        big_b = e(nc.sbuf_tensor("big_b", [128, 32768], BF16))
        k_sb = e(nc.sbuf_tensor("k_sb", [128, T], BF16))
        v_sb = e(nc.sbuf_tensor("v_sb", [128, T], BF16))
        ident = e(nc.sbuf_tensor("ident_s", [128, 128], BF16))
        ones_c = e(nc.sbuf_tensor("ones_c", [128, 1], BF16))
        ones_r = e(nc.sbuf_tensor("ones_r", [1, 128], F32))
        gs_sb = e(nc.sbuf_tensor("gs_sb", [128, 64], F32))
        us_sb = e(nc.sbuf_tensor("us_sb", [128, 64], F32))
        ds_sb = e(nc.sbuf_tensor("ds_sb", [128, 16], F32))
        scr4 = e(nc.sbuf_tensor("scr4", [128, 2048], F32))
        cb15 = e(nc.sbuf_tensor("cb15", [128, 1], F32))

        ba = big_a.ap()
        bb = big_b.ap()
        ba3 = ba.rearrange("p (c t) -> p c t", c=16)
        bbh = bb.rearrange("p (h t) -> p h t", h=16)

        def xnT(dc, lo, hi):
            return ba[:, dc * 2048 + lo: dc * 2048 + hi]

        def qcols(h, lo, hi):
            return bb[:, h * 2048 + lo: h * 2048 + hi]

        ld_ident = S.dma("sync", qS, ident.ap(), ident_d)
        S.dma("sync", qS, ones_c.ap(), onescol_d)
        S.dma("sync", qS, ones_r.ap(), onesrow_d)
        S.dma("sync", qS, gs_sb.ap(), gs_d)
        S.dma("sync", qS, us_sb.ap(), us_d)
        ld_consts = S.dma("sync", qS, ds_sb.ap(), ds_d)

        tr_evac = {}
        tr_done = {}
        cb15_v = S.op("vector", lambda e_, a=cb15.ap(): e_.memset(a, CBIAS), vE)

        xn_evac = [0] * TC
        with ExitStack() as ps01_stack:
            ep = ps01_stack.enter_context
            qkv_ps = [ep(nc.psum_tensor(f"qkv_ps{i}", [128, 512], F32)) for i in range(2)]
            tr_ps = [ep(nc.psum_tensor(f"tr_ps{i}", [128, 1024], BF16)) for i in range(2)]

            with ExitStack() as p0:
                e0 = p0.enter_context
                xbuf = [e0(nc.sbuf_tensor(f"xbuf{i}", [128, 2048], F32)) for i in range(2)]
                sqbuf = e0(nc.sbuf_tensor("sqbuf", [128, 2048], F32))
                xn_tok = e0(nc.sbuf_tensor("xn_tok", [128, 2048], BF16))
                stat = e0(nc.sbuf_tensor("stat", [128, 224], F32))

                x_ld = [0] * TC
                xn_pass = [0] * TC
                red_done = [0] * TC
                sq_done = [0] * TC
                st = stat.ap()
                last_tr_of_tc = 0
                stat_ms = S.op("vector", lambda e_, a=stat.ap(): e_.memset(a, 0.0), vE)

                for tc in range(TC):
                    xb = xbuf[tc % 2].ap()
                    if tc >= 2:
                        S.wait("sync", vE, xn_pass[tc - 2])
                        S.wait("sync", sE, sq_done[tc - 2])
                    x_ld[tc] = S.dma("sync", qS, xb, x_d[tc * 128:(tc + 1) * 128, :])

                    def c(q, i=tc):
                        return st[:, q * 16 + i: q * 16 + i + 1]

                    S.wait("vector", qS, x_ld[tc])
                    red_done[tc] = S.op("vector", lambda e_, o=c(0), i=xb: e_.reduce_sum(o, i, axis=AX.X), vE)
                    S.wait("scalar", qS, x_ld[tc])
                    if tc == 0:
                        S.wait("scalar", vE, stat_ms)
                    sq_v = sq_done[tc] = S.op("scalar", lambda e_, i=xb, o=sqbuf.ap(), a=c(1): e_.activation(
                        o, i, AF.Square, accum_out=a), sE)
                    S.wait("vector", sE, sq_v)
                    S.wait("vector", vE, red_done[tc])
                    v1 = S.op("vector", lambda e_, o=c(2), i=c(0): e_.tensor_scalar(o, i, -1.0 / 2048.0, None, ALU.mult), vE)
                    v2 = S.op("vector", lambda e_, o=c(3), i=c(1): e_.tensor_scalar(o, i, 1.0 / 2048.0, None, ALU.mult), vE)
                    S.wait("vector", vE, v1)
                    v3 = S.op("vector", lambda e_, o=c(9), i=c(2): e_.tensor_scalar(o, i, i, None, ALU.mult), vE)
                    S.wait("vector", vE, v3)
                    v4 = S.op("vector", lambda e_, o=c(4), i=c(3), j=c(9): e_.tensor_sub(o, i, j), vE)
                    S.wait("vector", vE, v4)
                    v5 = S.op("vector", lambda e_, o=c(10), i=c(4): e_.tensor_scalar(o, i, EPS, None, ALU.add), vE)
                    S.wait("vector", vE, v5)
                    v_rv = S.op("vector", lambda e_, o=c(5), i=c(10): e_.reciprocal(o, i), vE)
                    S.wait("scalar", vE, v_rv)
                    s_rstd = S.op("scalar", lambda e_, o=c(6), i=c(5): e_.activation(o, i, AF.Sqrt), sE)
                    S.wait("vector", sE, s_rstd)
                    v7 = S.op("vector", lambda e_, o=c(8), i=c(6): e_.tensor_scalar(o, i, i, None, ALU.mult), vE)
                    S.wait("vector", vE, v7)
                    v8 = S.op("vector", lambda e_, o=c(11), i=c(8), a=c(10): e_.tensor_scalar(o, i, a, None, ALU.mult), vE)
                    S.wait("vector", vE, v8)
                    v9 = S.op("vector", lambda e_, o=c(12), i=c(11): e_.tensor_scalar(o, i, -0.5, 1.5, ALU.mult, ALU.add), vE)
                    S.wait("vector", vE, v9)
                    v10 = S.op("vector", lambda e_, o=c(7), i=c(6), a=c(12): e_.tensor_scalar(o, i, a, None, ALU.mult), vE)
                    S.wait("vector", vE, v10)
                    S.wait("vector", tE, last_tr_of_tc)
                    xn_pass[tc] = S.op("vector", lambda e_, o=xn_tok.ap(), i=xb, m=c(2), r=c(7):
                                       e_.tensor_scalar(o, i, m, r, ALU.add, ALU.mult), vE)

                    for half in range(2):
                        slot = half
                        S.wait("tensor", vE, xn_pass[tc])
                        if tc == 0 and half == 0:
                            S.wait("tensor", qS, ld_ident)
                        prev = tr_evac.get(slot)
                        if prev:
                            S.wait("tensor", sE, prev)
                        for j in range(8):
                            dc = half * 8 + j
                            v = S.op("tensor", lambda e_, o=tr_ps[slot].ap()[:, j * 128:(j + 1) * 128],
                                     i=xn_tok.ap()[:, dc * 128:(dc + 1) * 128], idn=ident.ap():
                                     e_.transpose(o, i, idn), tE if j == 7 else None)
                        tr_done[slot] = v
                        last_tr_of_tc = v
                        S.wait("scalar", tE, tr_done[slot])
                        dst = ba3[:, half * 8:(half + 1) * 8, tc * 128:(tc + 1) * 128]
                        src = tr_ps[slot].ap().rearrange("p (c t) -> p c t", c=8)
                        xn_evac[tc] = tr_evac[slot] = S.op(
                            "scalar", lambda e_, o=dst, i=src: e_.copy(o, i), sE)

                xn_sp_w = 0
                for dc in range(DCH):
                    S.wait("sync", sE, xn_evac[TC - 1])
                    xn_sp_w = S.dma("sync", qS, xn_sp[dc * 128:(dc + 1) * 128, :], xnT(dc, 0, 2048))
                p0_tE_end = S.cur(tE)
                p0_vE_end = S.cur(vE)
                p0_sE_end = S.cur(sE)

            with ExitStack() as p1:
                e1 = p1.enter_context
                cos_sb = e1(nc.sbuf_tensor("cos_sb", [128, T], F32))
                sin_sb = e1(nc.sbuf_tensor("sin_sb", [128, T], F32))
                wat = [e1(nc.sbuf_tensor(f"wat{i}", [128, 4096], BF16)) for i in range(2)]
                qt4 = e1(nc.sbuf_tensor("qt4", [128, 1024], F32))
                tcos = e1(nc.sbuf_tensor("tcos", [128, 1024], F32))
                tsin = e1(nc.sbuf_tensor("tsin", [128, 1024], BF16))
                roped = e1(nc.sbuf_tensor("roped", [128, 1024], BF16))

                S.wait("sync", vE, p0_vE_end)
                S.wait("sync", sE, p0_sE_end)
                S.wait("sync", tE, p0_tE_end)
                S.dma("sync", qS, cos_sb.ap(), cos_d)
                ld_tab = S.dma("sync", qS, sin_sb.ap(), sin_d)
                S.wait("gpsimd", vE, p0_vE_end)
                S.wait("gpsimd", sE, p0_sE_end)
                S.wait("gpsimd", tE, p0_tE_end)

                wsrc = wattn_d.rearrange("(c p) f -> p c f", p=128)
                wa3 = [w.ap().rearrange("p (c f) -> p c f", c=16) for w in wat]
                wat_ld = [0] * NSTRIP
                wat_fwd = [0] * NSTRIP
                strip_last_mm = [0] * NSTRIP
                ps_free = [None, None]
                rope_v = {}
                evac_last = {}
                gi = 0

                for s_ in range(NSTRIP):
                    slot = s_ % 2
                    if s_ >= 2:
                        S.wait("gpsimd", tE, strip_last_mm[s_ - 2])
                    for hf in range(2):
                        wat_ld[s_] = S.dma("gpsimd", qF, wa3[slot][:, hf * 8:(hf + 1) * 8, :],
                                           wsrc[:, hf * 8:(hf + 1) * 8, s_ * 256:(s_ + 1) * 256])
                    if s_ < 2:
                        wat_fwd[s_] = S.fwd("gpsimd", qF, wat_ld[s_], fG)
                    for tcb in range(4):
                        for tci in range(4):
                            tc = tcb * 4 + tci
                            pslot = gi % 2
                            gi += 1
                            ps = qkv_ps[pslot].ap()[:, 0:256]
                            if s_ < 2:
                                S.wait("tensor", fG, wat_fwd[s_])
                            else:
                                S.wait("tensor", qF, wat_ld[s_])
                            S.wait("tensor", sE, xn_evac[tc])
                            if ps_free[pslot] is not None:
                                S.wait("tensor", sE, ps_free[pslot])
                            for dc in range(DCH):
                                last = dc == DCH - 1
                                v = S.op("tensor", lambda e_, o=ps,
                                         l=xnT(dc, tc * 128, (tc + 1) * 128),
                                         r=wa3[slot][:, dc, :],
                                         st_=(dc == 0), sp_=last:
                                         e_.matmul(o, l, r, start=st_, stop=sp_),
                                         tE if last else None)
                            strip_last_mm[s_] = v
                            S.wait("scalar", tE, v)
                            if tci == 0 and (s_, tcb - 1) in rope_v:
                                S.wait("scalar", vE, rope_v[(s_, tcb - 1)])
                            if tci == 0 and tcb == 0 and (s_ - 1, 3) in rope_v:
                                S.wait("scalar", vE, rope_v[(s_ - 1, 3)])
                            ev = S.op("scalar", lambda e_, o=qt4.ap()[:, tci * 256:(tci + 1) * 256],
                                      i=ps: e_.copy(o, i), sE)
                            ps_free[pslot] = ev
                            evac_last[(s_, tcb)] = ev

                        S.wait("vector", sE, evac_last[(s_, tcb)])
                        if s_ == 0 and tcb == 0:
                            S.wait("vector", qS, ld_tab)
                        cosb = cos_sb.ap()[:, tcb * 512:(tcb + 1) * 512]
                        sinb = sin_sb.ap()[:, tcb * 512:(tcb + 1) * 512]
                        c4 = cosb.rearrange("p (t j) -> p t j", t=4)
                        s4 = sinb.rearrange("p (t j) -> p t j", t=4)
                        if s_ < 8:
                            q4 = qt4.ap().rearrange("p (t h j) -> p t h j", t=4, h=2)
                            t4c = tcos.ap().rearrange("p (t h j) -> p t h j", t=4, h=2)
                            t4s = tsin.ap().rearrange("p (t h j) -> p t h j", t=4, h=2)
                            r4 = roped.ap().rearrange("p (h t j) -> p t h j", h=2, t=4)
                            S.op("vector", lambda e_, o=t4c, a=q4,
                                 b=c4.unsqueeze(2).broadcast_to([128, 4, 2, 128]): e_.tensor_mul(o, a, b))
                            S.op("vector", lambda e_, o=t4s[:, :, :, 0:64], a=q4[:, :, :, 64:128],
                                 b=s4[:, :, 0:64].unsqueeze(2).broadcast_to([128, 4, 2, 64]):
                                 e_.tensor_mul(o, a, b))
                            vm = S.op("vector", lambda e_, o=t4s[:, :, :, 64:128], a=q4[:, :, :, 0:64],
                                      b=s4[:, :, 64:128].unsqueeze(2).broadcast_to([128, 4, 2, 64]):
                                      e_.tensor_mul(o, a, b), vE)
                            S.wait("vector", vE, vm)
                            prev_t = tr_done.get(2)
                            if prev_t:
                                S.wait("vector", tE, prev_t)
                            rope_v[(s_, tcb)] = S.op("vector", lambda e_, o=r4, a=t4c, b=t4s:
                                                     e_.tensor_add(o, a, b), vE)
                            slot_t = tcb % 2
                            S.wait("tensor", vE, rope_v[(s_, tcb)])
                            prev = tr_evac.get(slot_t)
                            if prev:
                                S.wait("tensor", sE, prev)
                            for j in range(8):
                                v = S.op("tensor", lambda e_, o=tr_ps[slot_t].ap()[:, j * 128:(j + 1) * 128],
                                         i=roped.ap()[:, j * 128:(j + 1) * 128], idn=ident.ap():
                                         e_.transpose(o, i, idn), tE if j == 7 else None)
                            tr_done[slot_t] = v
                            tr_done[2] = v
                            S.wait("scalar", tE, v)
                            dstq = bbh[:, s_ * 2:(s_ + 1) * 2, tcb * 512:(tcb + 1) * 512]
                            srcq = tr_ps[slot_t].ap().rearrange("p (h t) -> p h t", h=2)
                            tr_evac[slot_t] = S.op("scalar", lambda e_, o=dstq, i=srcq: e_.copy(o, i), sE)
                        else:
                            k4 = qt4.ap().rearrange("p (t c j) -> p t c j", t=4, c=2)[:, :, 0, :]
                            t4ck = tcos.ap()[:, 0:512].rearrange("p (t j) -> p t j", t=4)
                            t4sk = tsin.ap()[:, 0:512].rearrange("p (t j) -> p t j", t=4)
                            r4k = roped.ap()[:, 0:512].rearrange("p (t j) -> p t j", t=4)
                            S.op("vector", lambda e_, o=t4ck, a=k4, b=c4: e_.tensor_mul(o, a, b))
                            S.op("vector", lambda e_, o=t4sk[:, :, 0:64], a=k4[:, :, 64:128],
                                 b=s4[:, :, 0:64]: e_.tensor_mul(o, a, b))
                            vm = S.op("vector", lambda e_, o=t4sk[:, :, 64:128], a=k4[:, :, 0:64],
                                      b=s4[:, :, 64:128]: e_.tensor_mul(o, a, b), vE)
                            S.wait("vector", vE, vm)
                            prev_t = tr_done.get(2)
                            if prev_t:
                                S.wait("vector", tE, prev_t)
                            rope_v[(s_, tcb)] = S.op("vector", lambda e_, o=r4k, a=t4ck, b=t4sk:
                                                     e_.tensor_add(o, a, b), vE)
                            slot_t = tcb % 2
                            S.wait("tensor", vE, rope_v[(s_, tcb)])
                            prev = tr_evac.get(slot_t)
                            if prev:
                                S.wait("tensor", sE, prev)
                            for j in range(4):
                                v = S.op("tensor", lambda e_, o=tr_ps[slot_t].ap()[:, j * 128:(j + 1) * 128],
                                         i=roped.ap()[:, j * 128:(j + 1) * 128], idn=ident.ap():
                                         e_.transpose(o, i, idn), tE if j == 3 else None)
                            tr_done[slot_t] = v
                            tr_done[2] = v
                            S.wait("scalar", tE, v)
                            tr_evac[slot_t] = S.op(
                                "scalar", lambda e_, o=k_sb.ap()[:, tcb * 512:(tcb + 1) * 512],
                                i=tr_ps[slot_t].ap()[:, 0:512]: e_.copy(o, i), sE)
                            for tci in range(4):
                                tc = tcb * 4 + tci
                                S.op("scalar", lambda e_, o=v_sb.ap()[:, tc * 128:(tc + 1) * 128],
                                     i=qt4.ap()[:, tci * 256 + 128:(tci + 1) * 256]: e_.copy(o, i),
                                     sE if tci == 3 else None)

                p1_tE_end = S.cur(tE)
                p1_sE_end = S.cur(sE)
                p1_vE_end = S.cur(vE)

        wg_cast = [0] * 32
        wu_cast = [0] * 32
        for sp in range(32):
            wg_cast[sp] = S.dma("gpsimd", qG, wg_b_d[sp], wg_i_d[sp])
        S.wait("gpsimd", tE, p1_tE_end)
        wout_ld = [0] * DCH
        wo_src = wout_d.rearrange("(c p) f -> p c f", p=128)
        for fc in range(DCH):
            wout_ld[fc] = S.dma("gpsimd", qG, ba[:, fc * 2048:(fc + 1) * 2048], wo_src[:, fc, :])
        wout_f = S.fwd("gpsimd", qG, wout_ld[DCH - 1], fG)
        for sp in range(32):
            wu_cast[sp] = S.dma("gpsimd", qG, wu_b_d[sp], wu_i_d[sp])
        gu_cast_f = S.fwd("gpsimd", qG, wu_cast[31], fG)

        dbg_qkv_dump = 0
        if debug is True:
            S.wait("sync", vE, p1_vE_end)
            S.wait("sync", sE, p1_sE_end)
            S.dma("sync", qS, dbg_q, bb)
            S.dma("sync", qS, dbg_k, k_sb.ap())
            dbg_qkv_dump = S.dma("sync", qS, dbg_v, v_sb.ap())

        with ExitStack() as p2:
            e2 = p2.enter_context
            attn_sb = e2(nc.sbuf_tensor("attn_sb", [128, 4096], BF16))
            r_sb = e2(nc.sbuf_tensor("r_sb", [128, 1024], F32))
            den_sb = e2(nc.sbuf_tensor("den_sb", [1, 1024], F32))
            prep = [e2(nc.sbuf_tensor(f"prep{i}", [128, 4096], BF16)) for i in range(2)]
            s_stage = [e2(nc.sbuf_tensor(f"s_stage{i}", [128, 2048], BF16)) for i in range(4)]

            wd_ts = [0] * 32
            wd_w = [0] * 32
            wd_w_f = [0] * 32
            cast_v = [0] * 33
            cast_v[0] = S.dma("gpsimd", qG, prep[0].ap(), wd_i_d[0])
            for sp in range(32):
                if sp < 31:
                    if sp >= 1:
                        S.wait("gpsimd", fS, wd_w_f[sp - 1])
                    cast_v[sp + 1] = S.dma("gpsimd", qG, prep[(sp + 1) % 2].ap(), wd_i_d[sp + 1])
                    S.wait("gpsimd", qG, cast_v[sp + 1])
                else:
                    S.wait("gpsimd", qG, cast_v[sp])
                wd_ts[sp] = S.op("gpsimd", lambda e_, a=prep[sp % 2].ap():
                                 e_.tensor_scalar(a, a, -128.0, None, ALU.add), fG)
                S.wait("sync", fG, wd_ts[sp])
                wd_w[sp] = S.dma("sync", qS, wd_b_d[sp], prep[sp % 2].ap())
                wd_w_f[sp] = S.fwd("sync", qS, wd_w[sp], fS)

            with ExitStack() as ps2:
                ep2 = ps2.enter_context
                sc_ps = [ep2(nc.psum_tensor(f"sc_ps{i}", [128, 1024], F32)) for i in range(2)]
                av_ps = ep2(nc.psum_tensor("av_ps", [128, 1024], F32))
                den_ps = ep2(nc.psum_tensor("den_ps", [128, 1024], F32))

                NKC = 16
                exp_inc = {}
                den_done = {}
                recip_inc = {}
                r_evac_inc = {}
                avtt_inc = {}

                for pi in range(32):
                    h, qh = pi // 2, pi % 2
                    qlo = qh * 1024

                    def emit_avden(kc):
                        aslot = kc % 4
                        S.wait("tensor", sE, exp_inc[(pi, kc)])
                        if kc == 0 and pi > 0:
                            S.wait("tensor", vE, avtt_inc[pi - 1])
                            S.wait("tensor", sE, r_evac_inc[pi - 1])
                        for half in range(2):
                            S.op("tensor", lambda e_, o=av_ps.ap()[:, half * 512:(half + 1) * 512],
                                 l=v_sb.ap()[:, kc * 128:(kc + 1) * 128],
                                 r=attn_sb.ap()[:, aslot * 1024 + half * 512: aslot * 1024 + (half + 1) * 512],
                                 st_=(kc == 0), sp_=(kc == NKC - 1):
                                 e_.matmul(o, l, r, start=st_, stop=sp_))
                        for half in range(2):
                            v = S.op("tensor", lambda e_, o=den_ps.ap()[0:1, half * 512:(half + 1) * 512],
                                     l=ones_c.ap(),
                                     r=attn_sb.ap()[:, aslot * 1024 + half * 512: aslot * 1024 + (half + 1) * 512],
                                     st_=(kc == 0), sp_=(kc == NKC - 1):
                                     e_.matmul(o, l, r, start=st_, stop=sp_),
                                     tE if half == 1 else None)
                        den_done[(pi, kc)] = v

                    for kc in range(NKC):
                        pslot = kc % 2
                        if kc >= 2:
                            S.wait("tensor", sE, exp_inc[(pi, kc - 2)])
                        elif kc == 0 and pi > 0:
                            S.wait("tensor", sE, exp_inc[(pi - 1, 14)])
                        elif kc == 1 and pi > 0:
                            S.wait("tensor", sE, exp_inc[(pi - 1, 15)])
                        if pi == 0 and kc == 0:
                            S.wait("tensor", sE, p1_sE_end)
                            S.wait("tensor", vE, p1_vE_end)
                            S.wait("tensor", qS, dbg_qkv_dump)
                            S.wait("scalar", vE, cb15_v)
                        for half in range(2):
                            v = S.op("tensor", lambda e_, o=sc_ps[pslot].ap()[:, half * 512:(half + 1) * 512],
                                     l=k_sb.ap()[:, kc * 128:(kc + 1) * 128],
                                     r=qcols(h, qlo + half * 512, qlo + (half + 1) * 512):
                                     e_.matmul(o, l, r, start=True, stop=True),
                                     tE if half == 1 else None)
                        sc_done = v
                        aslot = kc % 4
                        S.wait("scalar", tE, sc_done)
                        if kc < 4:
                            if pi > 0:
                                S.wait("scalar", tE, den_done[(pi - 1, 12 + kc)])
                        else:
                            S.wait("scalar", tE, den_done[(pi, kc - 4)])
                        exp_inc[(pi, kc)] = S.op(
                            "scalar", lambda e_, o=attn_sb.ap()[:, aslot * 1024:(aslot + 1) * 1024],
                            i=sc_ps[pslot].ap(), b_=cb15.ap(): e_.activation(o, i, AF.Exp, bias=b_, scale=INVSQ), sE)
                        if kc >= 2:
                            emit_avden(kc - 2)
                    emit_avden(NKC - 2)
                    emit_avden(NKC - 1)
                    S.wait("vector", tE, den_done[(pi, 15)])
                    recip_inc[pi] = S.op("vector", lambda e_, o=den_sb.ap(), i=den_ps.ap()[0:1, :]:
                                         e_.reciprocal(o, i), vE)
                    S.wait("tensor", vE, recip_inc[pi])
                    for half in range(2):
                        v = S.op("tensor", lambda e_, o=den_ps.ap()[:, half * 512:(half + 1) * 512],
                                 l=ones_r.ap(), r=den_sb.ap()[:, half * 512:(half + 1) * 512]:
                                 e_.matmul(o, l, r, start=True, stop=True),
                                 tE if half == 1 else None)
                    r_done = v
                    S.wait("scalar", tE, r_done)
                    r_evac_inc[pi] = S.op("scalar", lambda e_, o=r_sb.ap(), i=den_ps.ap():
                                          e_.copy(o, i), sE)
                    S.wait("vector", sE, r_evac_inc[pi])
                    avtt_inc[pi] = S.op("vector", lambda e_, o=qcols(h, qlo, qlo + 1024),
                                        a=av_ps.ap(), b=r_sb.ap(): e_.tensor_mul(o, a, b), vE)

                p2_vE_end = S.cur(vE)

            dbg_av_dump = 0
            if debug is True:
                S.wait("sync", vE, p2_vE_end)
                dbg_av_dump = S.dma("sync", qS, dbg_av, bb)
            with ExitStack() as ps3:
                ep3 = ps3.enter_context
                po_ps = [ep3(nc.psum_tensor(f"po_ps{i}", [128, 512], F32)) for i in range(2)]

                po_evac = {}
                s_store = [0] * DCH
                s_store_f = [0] * DCH
                g3 = 0
                for ec in range(DCH):
                    for st4 in range(4):
                        pslot = g3 % 2
                        if g3 >= 2:
                            S.wait("tensor", sE, po_evac[g3 - 2])
                        if g3 == 0:
                            S.wait("tensor", vE, p2_vE_end)
                            S.wait("tensor", fG, wout_f)
                        for fc in range(DCH):
                            last = fc == DCH - 1
                            v = S.op("tensor", lambda e_, o=po_ps[pslot].ap(),
                                     l=ba[:, fc * 2048 + ec * 128: fc * 2048 + (ec + 1) * 128],
                                     r=qcols(fc, st4 * 512, (st4 + 1) * 512),
                                     st_=(fc == 0), sp_=last:
                                     e_.matmul(o, l, r, start=st_, stop=sp_), tE if last else None)
                        S.wait("scalar", tE, v)
                        if ec >= 4 and st4 == 0:
                            S.wait("scalar", fG, s_store_f[ec - 4])
                        po_evac[g3] = S.op(
                            "scalar", lambda e_, o=s_stage[ec % 4].ap()[:, st4 * 512:(st4 + 1) * 512],
                            i=po_ps[pslot].ap(): e_.copy(o, i), sE)
                        g3 += 1
                    S.wait("gpsimd", sE, po_evac[g3 - 1])
                    s_store[ec] = S.dma("gpsimd", qG, s_sp[ec * 128:(ec + 1) * 128, :],
                                        s_stage[ec % 4].ap())
                    s_store_f[ec] = S.fwd("gpsimd", qG, s_store[ec], fG)
                p3_tE_end = S.cur(tE)
                p3_sE_end = S.cur(sE)
                p3_qG_s_last = s_store[DCH - 1]

        with ExitStack() as p4:
            e4 = p4.enter_context
            x_tile = [e4(nc.sbuf_tensor(f"x_tile{i}", [128, 1024], F32)) for i in range(2)]
            out_sb = [e4(nc.sbuf_tensor(f"out_sb{i}", [128, 2048], F32)) for i in range(2)]
            comb_sb = e4(nc.sbuf_tensor("comb_sb", [128, 8192], BF16))
            wd_buf = [e4(nc.sbuf_tensor(f"wd_buf{i}", [128, 4096], BF16)) for i in range(2)]

            with ExitStack() as ps4:
                ep4 = ps4.enter_context
                g_ps = [ep4(nc.psum_tensor(f"g_ps{i}", [128, 512], F32)) for i in range(2)]
                u_ps = [ep4(nc.psum_tensor(f"u_ps{i}", [128, 512], F32)) for i in range(2)]
                d_ps = [ep4(nc.psum_tensor(f"d_ps{i}", [128, 512], F32)) for i in range(2)]
                tr4 = [ep4(nc.psum_tensor(f"tr4_{i}", [128, 1024], BF16)) for i in range(2)]

                wg_buf = [ba[:, 0:4096], ba[:, 4096:8192]]
                wu_buf = [ba[:, 8192:12288], ba[:, 12288:16384]]
                xn_blk = ba[:, 16384:24576]
                s_blk = ba[:, 24576:32768]
                hmid = bb

                xnsp3 = xn_sp.rearrange("(c p) t -> p c t", p=128)
                ssp3 = s_sp.rearrange("(c p) t -> p c t", p=128)
                xnb3 = xn_blk.rearrange("p (c t) -> p c t", c=16)
                sb3 = s_blk.rearrange("p (c t) -> p c t", c=16)
                cb3 = comb_sb.ap().rearrange("p (c t) -> p c t", c=16)

                dbg3_gates = {}
                gu_mm_done = [0] * NBLK
                tp_done = [0] * NBLK
                stt_done = {}
                comb_done = {}
                silu_done = {}
                dsilu_done = {}
                dn_grp = {}
                tt_add = {}
                out_store = {}
                gu_group = {}

                dbg_hm_dump = 0
                for b in range(NBLK):
                    if debug is True and b == 1:
                        S.wait("sync", vE, stt_done[(0, 63)])
                        S.wait("sync", tE, dn_grp[(0, 31)])
                        dbg_hm_dump = S.dma("sync", qS, dbg_hm, bb)
                    if b >= 1:
                        S.wait("sync", tE, gu_mm_done[b - 1])
                    else:
                        S.wait("sync", tE, p3_tE_end)
                    S.wait("sync", qS, xn_sp_w)
                    xn_ld = 0
                    for hf in range(2):
                        xn_ld = S.dma("sync", qD, xnb3[:, hf * 8:(hf + 1) * 8, :],
                                      xnsp3[:, hf * 8:(hf + 1) * 8, b * 512:(b + 1) * 512])
                    xn_fwd = S.fwd("sync", qD, xn_ld, fS)
                    if b >= 1:
                        S.wait("gpsimd", vE, comb_done[(b - 1, 7)])
                        if debug == 3 and 1 in dbg3_gates:
                            S.wait("gpsimd", qS, dbg3_gates[1])
                    else:
                        S.wait("gpsimd", tE, p3_tE_end)
                    S.wait("gpsimd", qG, s_store[DCH - 1])
                    s_ld = 0
                    for hf in range(2):
                        s_ld = S.dma("gpsimd", qH, sb3[:, hf * 8:(hf + 1) * 8, :],
                                     ssp3[:, hf * 8:(hf + 1) * 8, b * 512:(b + 1) * 512])
                    s_fwd = S.fwd("gpsimd", qH, s_ld, fG)

                    for sp in range(32):
                        slot = sp % 2
                        if sp >= 2:
                            S.wait("sync", tE, gu_group[(b, sp * 2 - 3)])
                        elif b >= 1:
                            S.wait("sync", tE, gu_mm_done[b - 1])
                            S.wait("sync", tE, tp_done[b - 1])
                            if debug == 3 and 1 in dbg3_gates:
                                S.wait("sync", qS, dbg3_gates[1])
                        else:
                            S.wait("sync", tE, p3_tE_end)
                        S.wait("sync", fG, gu_cast_f)
                        wg_ld = S.dma("sync", qA, wg_buf[slot], wg_b_d[sp])
                        if sp < 2:
                            wg_fwd = S.fwd("sync", qA, wg_ld, fS)
                        S.wait("sync", fG, gu_cast_f)
                        wu_ld = S.dma("sync", qB, wu_buf[slot], wu_b_d[sp])
                        if sp < 2:
                            wu_fwd = S.fwd("sync", qB, wu_ld, fS)
                        for ici in range(2):
                            ic = sp * 2 + ici
                            pslot = ic % 2
                            if sp < 2:
                                S.wait("tensor", fS, wg_fwd)
                            else:
                                S.wait("tensor", qA, wg_ld)
                            if b == 0 and ic == 0:
                                S.wait("tensor", fS, xn_fwd)
                                S.wait("tensor", qS, ld_consts)
                                S.wait("tensor", sE, p3_sE_end)
                            if ic >= 2:
                                S.wait("tensor", sE, silu_done[(b, ic - 2)])
                            elif b >= 1:
                                S.wait("tensor", sE, silu_done[(b - 1, 62 + ic)])
                            for hc in range(DCH):
                                v = S.op("tensor", lambda e_, o=g_ps[pslot].ap(),
                                         l=wg_buf[slot][:, (ici * 16 + hc) * 128:(ici * 16 + hc + 1) * 128],
                                         r=xnb3[:, hc, :], st_=(hc == 0), sp_=(hc == DCH - 1):
                                         e_.matmul(o, l, r, start=st_, stop=sp_),
                                         tE if hc == DCH - 1 else None)
                            g_done = v
                            if sp < 2:
                                S.wait("tensor", fS, wu_fwd)
                            else:
                                S.wait("tensor", qB, wu_ld)
                            if ic >= 2:
                                S.wait("tensor", vE, stt_done[(b, ic - 2)])
                            elif b >= 1:
                                S.wait("tensor", vE, stt_done[(b - 1, 62 + ic)])
                            if debug and b == 1 and ic == 0:
                                S.wait("vector", qS, dbg_hm_dump)
                            for hc in range(DCH):
                                v = S.op("tensor", lambda e_, o=u_ps[pslot].ap(),
                                         l=wu_buf[slot][:, (ici * 16 + hc) * 128:(ici * 16 + hc + 1) * 128],
                                         r=xnb3[:, hc, :], st_=(hc == 0), sp_=(hc == DCH - 1):
                                         e_.matmul(o, l, r, start=st_, stop=sp_),
                                         tE if hc == DCH - 1 else None)
                            u_done = v
                            gu_group[(b, ic)] = u_done
                            S.wait("scalar", tE, g_done)
                            if ic >= 2:
                                S.wait("scalar", vE, stt_done[(b, ic - 2)])
                            elif b >= 1:
                                S.wait("scalar", vE, stt_done[(b - 1, 62 + ic)])
                            silu_done[(b, ic)] = S.op(
                                "scalar", lambda e_, o=scr4.ap()[:, pslot * 512:(pslot + 1) * 512],
                                i=g_ps[pslot].ap(), sc=gs_sb.ap()[:, ic:ic + 1]:
                                e_.activation(o, i, AF.Silu, scale=sc), sE)
                            S.wait("vector", tE, u_done)
                            S.wait("vector", sE, silu_done[(b, ic)])
                            if debug is True and b == 0 and ic == 0:
                                S.wait("vector", qS, dbg_av_dump)
                            stt_done[(b, ic)] = S.op(
                                "vector", lambda e_, o=hmid[:, ic * 512:(ic + 1) * 512],
                                i0=u_ps[pslot].ap(), sc=us_sb.ap()[:, ic:ic + 1],
                                i1=scr4.ap()[:, pslot * 512:(pslot + 1) * 512]:
                                e_.scalar_tensor_tensor(o, i0, sc, i1, ALU.mult, ALU.mult), vE)
                    gu_mm_done[b] = gu_group[(b, 63)]
                    if debug is True and b == 1:
                        pass

                    for sw in range(8):
                        for sp4 in range(4):
                            gsp = sw * 4 + sp4
                            dslot = gsp % 2
                            if gsp >= 2:
                                S.wait("sync", tE, dn_grp[(b, gsp - 2)])
                            elif b >= 1:
                                S.wait("sync", tE, dn_grp[(b - 1, 30 + gsp)])
                            else:
                                S.wait("sync", tE, p3_tE_end)
                                S.wait("sync", sE, p3_sE_end)
                                S.wait("sync", qS, wd_w[31])
                                S.wait("sync", qG, p3_qG_s_last)
                            S.wait("sync", qS, wd_w[gsp])
                            wd_ld = S.dma("sync", qC, wd_buf[dslot].ap(), wd_b_d[gsp])
                            wd_fwd = S.fwd("sync", qC, wd_ld, fS)
                            S.wait("tensor", fS, wd_fwd)
                            for ici in range(16):
                                ic = sp4 * 16 + ici
                                S.wait("tensor", vE, stt_done[(b, 63)] if sw == 0 else stt_done[(b, ic)])
                                if ic == 0 and sw > 0:
                                    S.wait("tensor", sE, dsilu_done[(b, sw - 1)])
                                elif ic == 0 and b >= 1:
                                    S.wait("tensor", sE, dsilu_done[(b - 1, 7)])
                                for hcs in range(2):
                                    v = S.op("tensor", lambda e_, o=d_ps[hcs].ap(),
                                             l=wd_buf[dslot].ap()[:, (ici * 2 + hcs) * 128:(ici * 2 + hcs + 1) * 128],
                                             r=hmid[:, ic * 512:(ic + 1) * 512],
                                             st_=(ic == 0), sp_=(ic == NIC - 1):
                                             e_.matmul(o, l, r, start=st_, stop=sp_),
                                             tE if hcs == 1 else None)
                            dn_grp[(b, gsp)] = v
                        if debug == 3 and sw == 1:
                            S.wait("scalar", qS, dbg_ga_dump)
                        for hcs in range(2):
                            hc = sw * 2 + hcs
                            S.wait("scalar", tE, dn_grp[(b, sw * 4 + 3)])
                            if sw >= 1:
                                S.wait("scalar", vE, comb_done[(b, sw - 1)])
                            elif b >= 1:
                                S.wait("scalar", vE, comb_done[(b - 1, 7)])
                            sv = S.op("scalar", lambda e_, o=scr4.ap()[:, 1024 + hcs * 512: 1536 + hcs * 512],
                                      i=d_ps[hcs].ap(), sc=ds_sb.ap()[:, hc:hc + 1]:
                                      e_.activation(o, i, AF.Silu, scale=sc), sE)
                            if hcs == 1:
                                dsilu_done[(b, sw)] = sv
                            S.wait("vector", sE, sv)
                            if sw == 0 and hcs == 0:
                                S.wait("vector", fG, s_fwd)
                            comb_done[(b, sw)] = S.op(
                                "vector", lambda e_, o=cb3[:, hc, :], a=sb3[:, hc, :],
                                b_=scr4.ap()[:, 1024 + hcs * 512: 1536 + hcs * 512]:
                                e_.tensor_mul(o, a, b_), vE)
                            if debug == 3 and sw == 0:
                                S.wait("sync", sE, sv)
                                dbg_ga_dump = S.dma("sync", qS, dbg_ga[b][:, hcs * 512:(hcs + 1) * 512],
                                                    scr4.ap()[:, 1024 + hcs * 512: 1536 + hcs * 512])

                    for tsub in range(4):
                        trow = b * 512 + tsub * 128
                        oslot = (b * 4 + tsub) % 2
                        for r in range(2):
                            key_prev = (b, tsub - 1, r) if tsub >= 1 else ((b - 1, 3, r) if b >= 1 else None)
                            if key_prev in tt_add:
                                S.wait("gpsimd", vE, tt_add[key_prev])
                            xt_ld = S.dma("gpsimd", qE, x_tile[r].ap(),
                                          x_d[trow:trow + 128, r * 1024:(r + 1) * 1024])
                            S.wait("tensor", vE, comb_done[(b, 7)])
                            if key_prev in tt_add:
                                S.wait("tensor", vE, tt_add[key_prev])
                            for j in range(8):
                                hc = r * 8 + j
                                v = S.op("tensor", lambda e_, o=tr4[r].ap()[:, j * 128:(j + 1) * 128],
                                         i=cb3[:, hc, tsub * 128:(tsub + 1) * 128], idn=ident.ap():
                                         e_.transpose(o, i, idn), tE if j == 7 else None)
                            tp_done[b] = v
                            S.wait("vector", tE, v)
                            S.wait("vector", qE, xt_ld)
                            pkey = (b * 4 + tsub) - 2
                            if pkey >= 0:
                                S.wait("vector", qO, out_store[pkey])
                            tt_add[(b, tsub, r)] = S.op(
                                "vector", lambda e_, o=out_sb[oslot].ap()[:, r * 1024:(r + 1) * 1024],
                                a=tr4[r].ap(), b_=x_tile[r].ap():
                                e_.tensor_add(o, a, b_), vE)
                        S.wait("gpsimd", vE, tt_add[(b, tsub, 1)])
                        out_store[b * 4 + tsub] = S.dma("gpsimd", qO, out_d[trow:trow + 128, :],
                                                        out_sb[oslot].ap())
                    if debug == 3:
                        S.wait("sync", tE, tp_done[b])
                        S.dma("sync", qS, dbg_cb[b], ba[:, 0:8192])
                        dbg_cb_dump = S.dma("sync", qS, dbg_sb[b], ba[:, 24576:32768])
                        if b == 0:
                            S.wait("sync", vE, stt_done[(0, 63)])
                            S.wait("sync", tE, dn_grp[(0, 31)])
                            dbg_hm_d3 = S.dma("sync", qS, dbg_hm, bb)
                            dbg3_gates[0] = dbg_hm_d3
                        dbg3_gates[1] = dbg_cb_dump

                S.wait("gpsimd", qE, S.cur(qE))
                S.wait("gpsimd", qH, S.cur(qH))
                S.wait("gpsimd", qO, S.cur(qO))
                S.wait("gpsimd", qG, S.cur(qG))
                S.wait("sync", qS, S.cur(qS))
                S.wait("sync", qA, S.cur(qA))
                S.wait("sync", qB, S.cur(qB))
                S.wait("sync", qC, S.cur(qC))
                S.wait("sync", qD, S.cur(qD))

        with nc.Block() as block:
            @block.sync
            def _(eng):
                for fn in S.ops["sync"]:
                    fn(eng)

            @block.gpsimd
            def _(eng):
                for fn in S.ops["gpsimd"]:
                    fn(eng)

            @block.tensor
            def _(eng):
                for fn in S.ops["tensor"]:
                    fn(eng)

            @block.vector
            def _(eng):
                for fn in S.ops["vector"]:
                    fn(eng)

            @block.scalar
            def _(eng):
                for fn in S.ops["scalar"]:
                    fn(eng)

    return nc


def _host_prep(L0_attn_x, L0_attn_w_attn, L0_attn_w_out, L0_ffn_gate_w, L0_ffn_gate_scale,
               L0_ffn_up_w, L0_ffn_up_scale, L0_ffn_down_w, L0_ffn_down_scale):
    def gu_img(w):
        a = np.ascontiguousarray(w.T)
        a = a.reshape(16, 128, 64, 128)
        a = a.transpose(2, 1, 0, 3)
        a = a.reshape(32, 2, 128, 16, 128).transpose(0, 2, 1, 3, 4)
        return np.ascontiguousarray(a.reshape(32, 128, 4096))

    def d_img(w):
        a = np.ascontiguousarray(w.T)
        a = a.reshape(4, 16, 128, 8, 2, 128)
        a = a.transpose(3, 0, 2, 1, 4, 5)
        return np.ascontiguousarray(a.reshape(32, 128, 4096))

    wg_img = gu_img(np.asarray(L0_ffn_gate_w))
    wu_img = gu_img(np.asarray(L0_ffn_up_w))
    wd_img = d_img(np.asarray(L0_ffn_down_w))

    gs = np.ascontiguousarray(np.asarray(L0_ffn_gate_scale, dtype=np.float32).reshape(64, 128).T)
    us = np.ascontiguousarray(np.asarray(L0_ffn_up_scale, dtype=np.float32).reshape(64, 128).T)
    ds = np.ascontiguousarray(np.asarray(L0_ffn_down_scale, dtype=np.float32).reshape(16, 128).T)

    jj = np.arange(0, 128, 2, dtype=np.float32)
    pos = np.float32(10000.0) ** ((np.float32(-2.0) * jj - np.float32(1.0)) / np.float32(128.0))
    tt = np.arange(T, dtype=np.float32)
    ang = (tt[:, None] * pos[None, :]).astype(np.float32)
    cos = np.cos(np.concatenate([ang, ang], axis=1)).astype(np.float32)
    sin = np.sin(np.concatenate([ang, ang], axis=1)).astype(np.float32)
    sin_s = np.concatenate([-sin[:, :64], sin[:, 64:]], axis=1)
    cos_t = np.ascontiguousarray(cos.reshape(16, 128, 128).transpose(1, 0, 2).reshape(128, T))
    sin_t = np.ascontiguousarray(sin_s.reshape(16, 128, 128).transpose(1, 0, 2).reshape(128, T))

    shared = dict(
        w_attn=np.ascontiguousarray(np.asarray(L0_attn_w_attn, dtype=np.float32)),
        w_out=np.ascontiguousarray(np.asarray(L0_attn_w_out, dtype=np.float32)),
        wg_img_i=wg_img, wu_img_i=wu_img, wd_img_i=wd_img,
        gscale=gs, uscale=us, dscale=ds,
        cos_t=cos_t, sin_t=sin_t,
        ident=np.eye(128, dtype=ml_dtypes.bfloat16),
        ones_col=np.ones((128, 1), dtype=ml_dtypes.bfloat16),
        ones_row=np.ones((1, 128), dtype=np.float32),
    )
    in_maps = []
    xa = np.asarray(L0_attn_x, dtype=np.float32)
    for c in range(8):
        m = dict(shared)
        m["x"] = np.ascontiguousarray(xa[c])
        in_maps.append(m)
    return in_maps


def kernel(**inputs):
    in_maps = _host_prep(**inputs)
    nc = build_program(debug=False)
    res = run_bass_kernel_spmd(nc, in_maps, core_ids=list(range(8)))
    return np.stack([np.asarray(r["out"], dtype=np.float32) for r in res.results], axis=0)


# version 16
# speedup vs baseline: 1.3349x; 1.2703x over previous
import numpy as np
import ml_dtypes
import concourse.bass as bass
import concourse.mybir as mybir
from concourse.bass_utils import run_bass_kernel_spmd
from contextlib import ExitStack

F32 = mybir.dt.float32
BF16 = mybir.dt.bfloat16
I32 = mybir.dt.int32
AF = mybir.ActivationFunctionType
ALU = mybir.AluOpType
AX = mybir.AxisListType

T = 2048
D = 2048
NH = 16
HD = 128
II = 8192
TC = 16
DCH = 16
NIC = 64
BLK = 512
NBLK = 4
NSTRIP = 9
INVSQ = float(np.float32(1.0) / np.sqrt(np.float32(128.0)))
CBIAS = -15.0
EPS = 1e-5

ENGS = ("sync", "gpsimd", "tensor", "vector", "scalar")


class Sched:
    def __init__(self):
        self.ops = {k: [] for k in ENGS}
        self.n = {}

    def cur(self, sem):
        return self.n.get(sem.name, 0)

    def _bump(self, sem, by):
        v = self.n.get(sem.name, 0) + by
        self.n[sem.name] = v
        return v

    def emit(self, eng, fn):
        self.ops[eng].append(fn)

    def wait(self, eng, sem, val):
        if val is None or val <= 0:
            return
        self.emit(eng, lambda e, s=sem, v=val: e.wait_ge(s, v))

    def dma(self, eng, sem, dst, src):
        v = self._bump(sem, 16)
        self.emit(eng, lambda e, d=dst, s=src, sm=sem: e.dma_start(d, s).then_inc(sm, 16))
        return v

    def op(self, eng, build, sem=None):
        if sem is None:
            self.emit(eng, lambda e, b=build: b(e))
            return None
        v = self._bump(sem, 1)
        self.emit(eng, lambda e, b=build, sm=sem: b(e).then_inc(sm, 1))
        return v

    def fwd(self, eng, qsem, qval, fsem):
        self.wait(eng, qsem, qval)
        v = self._bump(fsem, 1)
        self.emit(eng, lambda e, s=fsem: e.sem_inc(s, 1))
        return v


def build_program(debug=False):
    nc = bass.Bass("TRN2", target_bir_lowering=False)

    x_d = nc.dram_tensor("x", [T, D], F32, kind="ExternalInput").ap()
    wattn_d = nc.dram_tensor("w_attn", [D, 2304], F32, kind="ExternalInput").ap()
    wout_d = nc.dram_tensor("w_out", [D, D], F32, kind="ExternalInput").ap()
    wg_i_d = nc.dram_tensor("wg_img_i", [32, 128, 4096], I32, kind="ExternalInput").ap()
    wu_i_d = nc.dram_tensor("wu_img_i", [32, 128, 4096], I32, kind="ExternalInput").ap()
    wd_i_d = nc.dram_tensor("wd_img_i", [32, 128, 4096], I32, kind="ExternalInput").ap()
    gs_d = nc.dram_tensor("gscale", [128, 64], F32, kind="ExternalInput").ap()
    us_d = nc.dram_tensor("uscale", [128, 64], F32, kind="ExternalInput").ap()
    ds_d = nc.dram_tensor("dscale", [128, 16], F32, kind="ExternalInput").ap()
    cos_d = nc.dram_tensor("cos_t", [128, T], F32, kind="ExternalInput").ap()
    sin_d = nc.dram_tensor("sin_t", [128, T], F32, kind="ExternalInput").ap()
    ident_d = nc.dram_tensor("ident", [128, 128], BF16, kind="ExternalInput").ap()
    onescol_d = nc.dram_tensor("ones_col", [128, 1], BF16, kind="ExternalInput").ap()
    onesrow_d = nc.dram_tensor("ones_row", [1, 128], F32, kind="ExternalInput").ap()

    out_d = nc.dram_tensor("out", [T, D], F32, kind="ExternalOutput").ap()

    iknd = "ExternalOutput" if debug else "Internal"
    xn_sp = nc.dram_tensor("xn_sp", [D, T], BF16, kind=iknd).ap()
    s_sp = nc.dram_tensor("s_sp", [D, T], BF16, kind=iknd).ap()
    wg_b_d = nc.dram_tensor("wg_img_b", [32, 128, 4096], BF16, kind=iknd).ap()
    wu_b_d = nc.dram_tensor("wu_img_b", [32, 128, 4096], BF16, kind=iknd).ap()
    wd_b_d = nc.dram_tensor("wd_img_b", [32, 128, 4096], BF16, kind=iknd).ap()
    if debug == 3:
        dbg_ga = nc.dram_tensor("dbg_ga", [4, 128, 1024], F32, kind="ExternalOutput").ap()
        dbg_cb = nc.dram_tensor("dbg_cb", [4, 128, 8192], BF16, kind="ExternalOutput").ap()
        dbg_hm = nc.dram_tensor("dbg_hm", [128, 32768], BF16, kind="ExternalOutput").ap()
        dbg_sb = nc.dram_tensor("dbg_sb", [4, 128, 8192], BF16, kind="ExternalOutput").ap()
    if debug is True:
        dbg_q = nc.dram_tensor("dbg_q", [128, 32768], BF16, kind="ExternalOutput").ap()
        dbg_k = nc.dram_tensor("dbg_k", [128, T], BF16, kind="ExternalOutput").ap()
        dbg_v = nc.dram_tensor("dbg_v", [128, T], BF16, kind="ExternalOutput").ap()
        dbg_av = nc.dram_tensor("dbg_av", [128, 32768], BF16, kind="ExternalOutput").ap()
        dbg_hm = nc.dram_tensor("dbg_hm", [128, 32768], BF16, kind="ExternalOutput").ap()

    S = Sched()

    with ExitStack() as top:
        e = top.enter_context
        qS = e(nc.semaphore("qS"))
        qG = e(nc.semaphore("qG"))
        qA = e(nc.semaphore("qA"))
        qB = e(nc.semaphore("qB"))
        qC = e(nc.semaphore("qC"))
        qD = e(nc.semaphore("qD"))
        qE = e(nc.semaphore("qE"))
        qF = e(nc.semaphore("qF"))
        qH = e(nc.semaphore("qH"))
        qO = e(nc.semaphore("qO"))
        fS = e(nc.semaphore("fS"))
        fG = e(nc.semaphore("fG"))
        qX0 = e(nc.semaphore("qX0"))
        qX1 = e(nc.semaphore("qX1"))
        qF1 = e(nc.semaphore("qF1"))
        qA1 = e(nc.semaphore("qA1"))
        qB1 = e(nc.semaphore("qB1"))
        qE1 = e(nc.semaphore("qE1"))
        qO1 = e(nc.semaphore("qO1"))
        tE = e(nc.semaphore("tE"))
        vE = e(nc.semaphore("vE"))
        sE = e(nc.semaphore("sE"))

        big_a = e(nc.sbuf_tensor("big_a", [128, 32768], BF16))
        big_b = e(nc.sbuf_tensor("big_b", [128, 32768], BF16))
        k_sb = e(nc.sbuf_tensor("k_sb", [128, T], BF16))
        v_sb = e(nc.sbuf_tensor("v_sb", [128, T], BF16))
        ident = e(nc.sbuf_tensor("ident_s", [128, 128], BF16))
        ones_c = e(nc.sbuf_tensor("ones_c", [128, 1], BF16))
        ones_r = e(nc.sbuf_tensor("ones_r", [1, 128], F32))
        gs_sb = e(nc.sbuf_tensor("gs_sb", [128, 64], F32))
        us_sb = e(nc.sbuf_tensor("us_sb", [128, 64], F32))
        ds_sb = e(nc.sbuf_tensor("ds_sb", [128, 16], F32))
        scr4 = e(nc.sbuf_tensor("scr4", [128, 2048], F32))
        cb15 = e(nc.sbuf_tensor("cb15", [128, 1], F32))

        ba = big_a.ap()
        bb = big_b.ap()
        ba3 = ba.rearrange("p (c t) -> p c t", c=16)
        bbh = bb.rearrange("p (h t) -> p h t", h=16)

        def xnT(dc, lo, hi):
            return ba[:, dc * 2048 + lo: dc * 2048 + hi]

        def qcols(h, lo, hi):
            return bb[:, h * 2048 + lo: h * 2048 + hi]

        ld_ident = S.dma("sync", qS, ident.ap(), ident_d)
        S.dma("sync", qS, ones_c.ap(), onescol_d)
        S.dma("sync", qS, ones_r.ap(), onesrow_d)
        S.dma("sync", qS, gs_sb.ap(), gs_d)
        S.dma("sync", qS, us_sb.ap(), us_d)
        ld_consts = S.dma("sync", qS, ds_sb.ap(), ds_d)
        consts_f = S.fwd("sync", qS, ld_consts, fS)

        tr_evac = {}
        tr_done = {}
        cb15_v = S.op("vector", lambda e_, a=cb15.ap(): e_.memset(a, CBIAS), vE)

        xn_evac = [0] * TC
        with ExitStack() as ps01_stack:
            ep = ps01_stack.enter_context
            qkv_ps = [ep(nc.psum_tensor(f"qkv_ps{i}", [128, 512], F32)) for i in range(2)]
            tr_ps = [ep(nc.psum_tensor(f"tr_ps{i}", [128, 1024], BF16)) for i in range(2)]

            with ExitStack() as p0:
                e0 = p0.enter_context
                xbuf = [e0(nc.sbuf_tensor(f"xbuf{i}", [128, 2048], F32)) for i in range(2)]
                sqbuf = e0(nc.sbuf_tensor("sqbuf", [128, 2048], F32))
                xn_tok = e0(nc.sbuf_tensor("xn_tok", [128, 2048], BF16))
                stat = e0(nc.sbuf_tensor("stat", [128, 224], F32))

                x_ld = [0] * TC
                xn_pass = [0] * TC
                red_done = [0] * TC
                sq_done = [0] * TC
                st = stat.ap()
                last_tr_of_tc = 0
                stat_ms = S.op("vector", lambda e_, a=stat.ap(): e_.memset(a, 0.0), vE)

                for tc in range(TC):
                    xb = xbuf[tc % 2].ap()
                    if tc >= 2:
                        S.wait("sync", vE, xn_pass[tc - 2])
                        S.wait("sync", sE, sq_done[tc - 2])
                    x_ld[tc] = S.dma("sync", [qX0, qX1][tc % 2], xb, x_d[tc * 128:(tc + 1) * 128, :])

                    def c(q, i=tc):
                        return st[:, q * 16 + i: q * 16 + i + 1]

                    S.wait("vector", [qX0, qX1][tc % 2], x_ld[tc])
                    red_done[tc] = S.op("vector", lambda e_, o=c(0), i=xb: e_.reduce_sum(o, i, axis=AX.X), vE)
                    S.wait("scalar", [qX0, qX1][tc % 2], x_ld[tc])
                    if tc == 0:
                        S.wait("scalar", vE, stat_ms)
                    sq_v = sq_done[tc] = S.op("scalar", lambda e_, i=xb, o=sqbuf.ap(), a=c(1): e_.activation(
                        o, i, AF.Square, accum_out=a), sE)
                    S.wait("vector", sE, sq_v)
                    S.wait("vector", vE, red_done[tc])
                    v1 = S.op("vector", lambda e_, o=c(2), i=c(0): e_.tensor_scalar(o, i, -1.0 / 2048.0, None, ALU.mult), vE)
                    v2 = S.op("vector", lambda e_, o=c(3), i=c(1): e_.tensor_scalar(o, i, 1.0 / 2048.0, None, ALU.mult), vE)
                    S.wait("vector", vE, v1)
                    v3 = S.op("vector", lambda e_, o=c(9), i=c(2): e_.tensor_scalar(o, i, i, None, ALU.mult), vE)
                    S.wait("vector", vE, v3)
                    v4 = S.op("vector", lambda e_, o=c(4), i=c(3), j=c(9): e_.tensor_sub(o, i, j), vE)
                    S.wait("vector", vE, v4)
                    v5 = S.op("vector", lambda e_, o=c(10), i=c(4): e_.tensor_scalar(o, i, EPS, None, ALU.add), vE)
                    S.wait("vector", vE, v5)
                    v_rv = S.op("vector", lambda e_, o=c(5), i=c(10): e_.reciprocal(o, i), vE)
                    S.wait("scalar", vE, v_rv)
                    s_rstd = S.op("scalar", lambda e_, o=c(6), i=c(5): e_.activation(o, i, AF.Sqrt), sE)
                    S.wait("vector", sE, s_rstd)
                    v7 = S.op("vector", lambda e_, o=c(8), i=c(6): e_.tensor_scalar(o, i, i, None, ALU.mult), vE)
                    S.wait("vector", vE, v7)
                    v8 = S.op("vector", lambda e_, o=c(11), i=c(8), a=c(10): e_.tensor_scalar(o, i, a, None, ALU.mult), vE)
                    S.wait("vector", vE, v8)
                    v9 = S.op("vector", lambda e_, o=c(12), i=c(11): e_.tensor_scalar(o, i, -0.5, 1.5, ALU.mult, ALU.add), vE)
                    S.wait("vector", vE, v9)
                    v10 = S.op("vector", lambda e_, o=c(7), i=c(6), a=c(12): e_.tensor_scalar(o, i, a, None, ALU.mult), vE)
                    S.wait("vector", vE, v10)
                    S.wait("vector", tE, last_tr_of_tc)
                    xn_pass[tc] = S.op("vector", lambda e_, o=xn_tok.ap(), i=xb, m=c(2), r=c(7):
                                       e_.tensor_scalar(o, i, m, r, ALU.add, ALU.mult), vE)

                    for half in range(2):
                        slot = half
                        S.wait("tensor", vE, xn_pass[tc])
                        if tc == 0 and half == 0:
                            S.wait("tensor", fS, consts_f)
                        prev = tr_evac.get(slot)
                        if prev:
                            S.wait("tensor", sE, prev)
                        for j in range(8):
                            dc = half * 8 + j
                            v = S.op("tensor", lambda e_, o=tr_ps[slot].ap()[:, j * 128:(j + 1) * 128],
                                     i=xn_tok.ap()[:, dc * 128:(dc + 1) * 128], idn=ident.ap():
                                     e_.transpose(o, i, idn), tE if j == 7 else None)
                        tr_done[slot] = v
                        last_tr_of_tc = v
                        S.wait("scalar", tE, tr_done[slot])
                        dst = ba3[:, half * 8:(half + 1) * 8, tc * 128:(tc + 1) * 128]
                        src = tr_ps[slot].ap().rearrange("p (c t) -> p c t", c=8)
                        xn_evac[tc] = tr_evac[slot] = S.op(
                            "scalar", lambda e_, o=dst, i=src: e_.copy(o, i), sE)

                xn_sp_w = 0
                for dc in range(DCH):
                    S.wait("sync", sE, xn_evac[TC - 1])
                    xn_sp_w = S.dma("sync", qS, xn_sp[dc * 128:(dc + 1) * 128, :], xnT(dc, 0, 2048))
                p0_tE_end = S.cur(tE)
                p0_vE_end = S.cur(vE)
                p0_sE_end = S.cur(sE)

            with ExitStack() as p1:
                e1 = p1.enter_context
                cos_sb = e1(nc.sbuf_tensor("cos_sb", [128, T], F32))
                sin_sb = e1(nc.sbuf_tensor("sin_sb", [128, T], F32))
                wat = [e1(nc.sbuf_tensor(f"wat{i}", [128, 4096], BF16)) for i in range(2)]
                qt4 = e1(nc.sbuf_tensor("qt4", [128, 1024], F32))
                tcos = e1(nc.sbuf_tensor("tcos", [128, 1024], F32))
                tsin = e1(nc.sbuf_tensor("tsin", [128, 1024], BF16))
                roped = e1(nc.sbuf_tensor("roped", [128, 1024], BF16))

                S.wait("sync", vE, p0_vE_end)
                S.wait("sync", sE, p0_sE_end)
                S.wait("sync", tE, p0_tE_end)
                S.dma("sync", qS, cos_sb.ap(), cos_d)
                ld_tab = S.dma("sync", qS, sin_sb.ap(), sin_d)
                tab_f = S.fwd("sync", qS, ld_tab, fS)
                S.wait("gpsimd", vE, p0_vE_end)
                S.wait("gpsimd", sE, p0_sE_end)
                S.wait("gpsimd", tE, p0_tE_end)

                wsrc = wattn_d.rearrange("(c p) f -> p c f", p=128)
                wa3 = [w.ap().rearrange("p (c f) -> p c f", c=16) for w in wat]
                wat_ld = [0] * NSTRIP
                wat_fwd = [0] * NSTRIP
                strip_last_mm = [0] * NSTRIP
                ps_free = [None, None]
                rope_v = {}
                evac_last = {}
                gi = 0

                for s_ in range(NSTRIP):
                    slot = s_ % 2
                    if s_ >= 2:
                        S.wait("gpsimd", tE, strip_last_mm[s_ - 2])
                    for hf in range(2):
                        wat_ld[s_] = S.dma("gpsimd", [qF, qF1][s_ % 2], wa3[slot][:, hf * 8:(hf + 1) * 8, :],
                                           wsrc[:, hf * 8:(hf + 1) * 8, s_ * 256:(s_ + 1) * 256])
                    if s_ < 2:
                        wat_fwd[s_] = S.fwd("gpsimd", qF, wat_ld[s_], fG)
                    for tcb in range(4):
                        for tci in range(4):
                            tc = tcb * 4 + tci
                            pslot = gi % 2
                            gi += 1
                            ps = qkv_ps[pslot].ap()[:, 0:256]
                            if s_ < 2:
                                S.wait("tensor", fG, wat_fwd[s_])
                            else:
                                S.wait("tensor", [qF, qF1][s_ % 2], wat_ld[s_])
                            S.wait("tensor", sE, xn_evac[tc])
                            if ps_free[pslot] is not None:
                                S.wait("tensor", sE, ps_free[pslot])
                            for dc in range(DCH):
                                last = dc == DCH - 1
                                v = S.op("tensor", lambda e_, o=ps,
                                         l=xnT(dc, tc * 128, (tc + 1) * 128),
                                         r=wa3[slot][:, dc, :],
                                         st_=(dc == 0), sp_=last:
                                         e_.matmul(o, l, r, start=st_, stop=sp_),
                                         tE if last else None)
                            strip_last_mm[s_] = v
                            S.wait("scalar", tE, v)
                            if tci == 0 and (s_, tcb - 1) in rope_v:
                                S.wait("scalar", vE, rope_v[(s_, tcb - 1)])
                            if tci == 0 and tcb == 0 and (s_ - 1, 3) in rope_v:
                                S.wait("scalar", vE, rope_v[(s_ - 1, 3)])
                            ev = S.op("scalar", lambda e_, o=qt4.ap()[:, tci * 256:(tci + 1) * 256],
                                      i=ps: e_.copy(o, i), sE)
                            ps_free[pslot] = ev
                            evac_last[(s_, tcb)] = ev

                        S.wait("vector", sE, evac_last[(s_, tcb)])
                        if s_ == 0 and tcb == 0:
                            S.wait("vector", fS, tab_f)
                        cosb = cos_sb.ap()[:, tcb * 512:(tcb + 1) * 512]
                        sinb = sin_sb.ap()[:, tcb * 512:(tcb + 1) * 512]
                        c4 = cosb.rearrange("p (t j) -> p t j", t=4)
                        s4 = sinb.rearrange("p (t j) -> p t j", t=4)
                        if s_ < 8:
                            q4 = qt4.ap().rearrange("p (t h j) -> p t h j", t=4, h=2)
                            t4c = tcos.ap().rearrange("p (t h j) -> p t h j", t=4, h=2)
                            t4s = tsin.ap().rearrange("p (t h j) -> p t h j", t=4, h=2)
                            r4 = roped.ap().rearrange("p (h t j) -> p t h j", h=2, t=4)
                            S.op("vector", lambda e_, o=t4c, a=q4,
                                 b=c4.unsqueeze(2).broadcast_to([128, 4, 2, 128]): e_.tensor_mul(o, a, b))
                            S.op("vector", lambda e_, o=t4s[:, :, :, 0:64], a=q4[:, :, :, 64:128],
                                 b=s4[:, :, 0:64].unsqueeze(2).broadcast_to([128, 4, 2, 64]):
                                 e_.tensor_mul(o, a, b))
                            vm = S.op("vector", lambda e_, o=t4s[:, :, :, 64:128], a=q4[:, :, :, 0:64],
                                      b=s4[:, :, 64:128].unsqueeze(2).broadcast_to([128, 4, 2, 64]):
                                      e_.tensor_mul(o, a, b), vE)
                            S.wait("vector", vE, vm)
                            prev_t = tr_done.get(2)
                            if prev_t:
                                S.wait("vector", tE, prev_t)
                            rope_v[(s_, tcb)] = S.op("vector", lambda e_, o=r4, a=t4c, b=t4s:
                                                     e_.tensor_add(o, a, b), vE)
                            slot_t = tcb % 2
                            S.wait("tensor", vE, rope_v[(s_, tcb)])
                            prev = tr_evac.get(slot_t)
                            if prev:
                                S.wait("tensor", sE, prev)
                            for j in range(8):
                                v = S.op("tensor", lambda e_, o=tr_ps[slot_t].ap()[:, j * 128:(j + 1) * 128],
                                         i=roped.ap()[:, j * 128:(j + 1) * 128], idn=ident.ap():
                                         e_.transpose(o, i, idn), tE if j == 7 else None)
                            tr_done[slot_t] = v
                            tr_done[2] = v
                            S.wait("scalar", tE, v)
                            dstq = bbh[:, s_ * 2:(s_ + 1) * 2, tcb * 512:(tcb + 1) * 512]
                            srcq = tr_ps[slot_t].ap().rearrange("p (h t) -> p h t", h=2)
                            tr_evac[slot_t] = S.op("scalar", lambda e_, o=dstq, i=srcq: e_.copy(o, i), sE)
                        else:
                            k4 = qt4.ap().rearrange("p (t c j) -> p t c j", t=4, c=2)[:, :, 0, :]
                            t4ck = tcos.ap()[:, 0:512].rearrange("p (t j) -> p t j", t=4)
                            t4sk = tsin.ap()[:, 0:512].rearrange("p (t j) -> p t j", t=4)
                            r4k = roped.ap()[:, 0:512].rearrange("p (t j) -> p t j", t=4)
                            S.op("vector", lambda e_, o=t4ck, a=k4, b=c4: e_.tensor_mul(o, a, b))
                            S.op("vector", lambda e_, o=t4sk[:, :, 0:64], a=k4[:, :, 64:128],
                                 b=s4[:, :, 0:64]: e_.tensor_mul(o, a, b))
                            vm = S.op("vector", lambda e_, o=t4sk[:, :, 64:128], a=k4[:, :, 0:64],
                                      b=s4[:, :, 64:128]: e_.tensor_mul(o, a, b), vE)
                            S.wait("vector", vE, vm)
                            prev_t = tr_done.get(2)
                            if prev_t:
                                S.wait("vector", tE, prev_t)
                            rope_v[(s_, tcb)] = S.op("vector", lambda e_, o=r4k, a=t4ck, b=t4sk:
                                                     e_.tensor_add(o, a, b), vE)
                            slot_t = tcb % 2
                            S.wait("tensor", vE, rope_v[(s_, tcb)])
                            prev = tr_evac.get(slot_t)
                            if prev:
                                S.wait("tensor", sE, prev)
                            for j in range(4):
                                v = S.op("tensor", lambda e_, o=tr_ps[slot_t].ap()[:, j * 128:(j + 1) * 128],
                                         i=roped.ap()[:, j * 128:(j + 1) * 128], idn=ident.ap():
                                         e_.transpose(o, i, idn), tE if j == 3 else None)
                            tr_done[slot_t] = v
                            tr_done[2] = v
                            S.wait("scalar", tE, v)
                            tr_evac[slot_t] = S.op(
                                "scalar", lambda e_, o=k_sb.ap()[:, tcb * 512:(tcb + 1) * 512],
                                i=tr_ps[slot_t].ap()[:, 0:512]: e_.copy(o, i), sE)
                            for tci in range(4):
                                tc = tcb * 4 + tci
                                S.op("scalar", lambda e_, o=v_sb.ap()[:, tc * 128:(tc + 1) * 128],
                                     i=qt4.ap()[:, tci * 256 + 128:(tci + 1) * 256]: e_.copy(o, i),
                                     sE if tci == 3 else None)

                p1_tE_end = S.cur(tE)
                p1_sE_end = S.cur(sE)
                p1_vE_end = S.cur(vE)

        wg_cast = [0] * 32
        wu_cast = [0] * 32
        for sp in range(32):
            wg_cast[sp] = S.dma("gpsimd", qG, wg_b_d[sp], wg_i_d[sp])
        for sp in range(32):
            wu_cast[sp] = S.dma("gpsimd", qG, wu_b_d[sp], wu_i_d[sp])
        gu_cast_f = S.fwd("gpsimd", qG, wu_cast[31], fG)

        S.wait("gpsimd", tE, p1_tE_end)
        wout_ld = [0] * DCH
        wo_src = wout_d.rearrange("(c p) f -> p c f", p=128)
        for fc in range(DCH):
            wout_ld[fc] = S.dma("gpsimd", qG, ba[:, fc * 2048:(fc + 1) * 2048], wo_src[:, fc, :])
        wout_f = S.fwd("gpsimd", qG, wout_ld[DCH - 1], fG)

        dbg_qkv_dump = 0
        if debug is True:
            S.wait("sync", vE, p1_vE_end)
            S.wait("sync", sE, p1_sE_end)
            S.dma("sync", qS, dbg_q, bb)
            S.dma("sync", qS, dbg_k, k_sb.ap())
            dbg_qkv_dump = S.dma("sync", qS, dbg_v, v_sb.ap())

        with ExitStack() as p2:
            e2 = p2.enter_context
            attn_sb = e2(nc.sbuf_tensor("attn_sb", [128, 4096], BF16))
            r_sb = e2(nc.sbuf_tensor("r_sb", [128, 1024], F32))
            den_sb = e2(nc.sbuf_tensor("den_sb", [1, 1024], F32))
            prep = [e2(nc.sbuf_tensor(f"prep{i}", [128, 4096], BF16)) for i in range(2)]
            s_stage = [e2(nc.sbuf_tensor(f"s_stage{i}", [128, 2048], BF16)) for i in range(4)]

            wd_cast = [0] * 32
            wd_cast_f = [0] * 32
            wd_w_f = [0] * 32
            wd_ts = [0] * 32
            wd_w = [0] * 32
            prep_free = [None, None]

            def emit_wd_prep(sp):
                slot = sp % 2
                if prep_free[slot] is not None:
                    S.wait("gpsimd", fS, prep_free[slot])
                wd_cast[sp] = S.dma("gpsimd", qG, prep[slot].ap(), wd_i_d[sp])
                wd_cast_f[sp] = S.fwd("gpsimd", qG, wd_cast[sp], fG)

            def emit_wd_ts(sp):
                slot = sp % 2
                S.wait("vector", fG, wd_cast_f[sp])
                wd_ts[sp] = S.op("vector", lambda e_, a=prep[slot].ap():
                                 e_.tensor_scalar(a, a, -128.0, None, ALU.add), vE)
                S.wait("sync", vE, wd_ts[sp])
                wd_w[sp] = S.dma("sync", qS, wd_b_d[sp], prep[slot].ap())
                wd_w_f[sp] = S.fwd("sync", qS, wd_w[sp], fS)
                prep_free[slot] = wd_w_f[sp]

            emit_wd_prep(0)
            emit_wd_prep(1)
            next_prep = 2
            next_ts = 0

            with ExitStack() as ps2:
                ep2 = ps2.enter_context
                sc_ps = [ep2(nc.psum_tensor(f"sc_ps{i}", [128, 1024], F32)) for i in range(2)]
                av_ps = ep2(nc.psum_tensor("av_ps", [128, 1024], F32))
                den_ps = ep2(nc.psum_tensor("den_ps", [1, 1024], F32))

                NKC = 16
                exp_inc = {}
                den_done = {}
                recip_inc = {}
                r_evac_inc = {}
                avtt_inc = {}

                for pi in range(32):
                    h, qh = pi // 2, pi % 2
                    qlo = qh * 1024

                    def emit_avden(kc):
                        aslot = kc % 4
                        S.wait("tensor", sE, exp_inc[(pi, kc)])
                        if kc == 0 and pi > 0:
                            S.wait("tensor", vE, avtt_inc[pi - 1])
                            S.wait("tensor", vE, recip_inc[pi - 1])
                        for half in range(2):
                            S.op("tensor", lambda e_, o=av_ps.ap()[:, half * 512:(half + 1) * 512],
                                 l=v_sb.ap()[:, kc * 128:(kc + 1) * 128],
                                 r=attn_sb.ap()[:, aslot * 1024 + half * 512: aslot * 1024 + (half + 1) * 512],
                                 st_=(kc == 0), sp_=(kc == NKC - 1):
                                 e_.matmul(o, l, r, start=st_, stop=sp_))
                        for half in range(2):
                            v = S.op("tensor", lambda e_, o=den_ps.ap()[:, half * 512:(half + 1) * 512],
                                     l=ones_c.ap(),
                                     r=attn_sb.ap()[:, aslot * 1024 + half * 512: aslot * 1024 + (half + 1) * 512],
                                     st_=(kc == 0), sp_=(kc == NKC - 1):
                                     e_.matmul(o, l, r, start=st_, stop=sp_),
                                     tE if half == 1 else None)
                        den_done[(pi, kc)] = v

                    for kc in range(NKC):
                        pslot = kc % 2
                        if kc >= 2:
                            S.wait("tensor", sE, exp_inc[(pi, kc - 2)])
                        elif kc == 0 and pi > 0:
                            S.wait("tensor", sE, r_evac_inc[pi - 1])
                        elif kc == 1 and pi > 0:
                            S.wait("tensor", sE, exp_inc[(pi - 1, 15)])
                        if pi == 0 and kc == 0:
                            S.wait("tensor", sE, p1_sE_end)
                            S.wait("tensor", vE, p1_vE_end)
                            S.wait("tensor", qS, dbg_qkv_dump)
                            S.wait("scalar", vE, cb15_v)
                        for half in range(2):
                            v = S.op("tensor", lambda e_, o=sc_ps[pslot].ap()[:, half * 512:(half + 1) * 512],
                                     l=k_sb.ap()[:, kc * 128:(kc + 1) * 128],
                                     r=qcols(h, qlo + half * 512, qlo + (half + 1) * 512):
                                     e_.matmul(o, l, r, start=True, stop=True),
                                     tE if half == 1 else None)
                        sc_done = v
                        aslot = kc % 4
                        S.wait("scalar", tE, sc_done)
                        if kc < 4:
                            if pi > 0:
                                S.wait("scalar", tE, den_done[(pi - 1, 12 + kc)])
                        else:
                            S.wait("scalar", tE, den_done[(pi, kc - 4)])
                        exp_inc[(pi, kc)] = S.op(
                            "scalar", lambda e_, o=attn_sb.ap()[:, aslot * 1024:(aslot + 1) * 1024],
                            i=sc_ps[pslot].ap(), b_=cb15.ap(): e_.activation(o, i, AF.Exp, bias=b_, scale=INVSQ), sE)
                        if kc >= 2:
                            emit_avden(kc - 2)
                    emit_avden(NKC - 2)
                    emit_avden(NKC - 1)
                    S.wait("vector", tE, den_done[(pi, 15)])
                    recip_inc[pi] = S.op("vector", lambda e_, o=den_sb.ap(), i=den_ps.ap():
                                         e_.reciprocal(o, i), vE)
                    S.wait("tensor", vE, recip_inc[pi])
                    S.wait("tensor", sE, exp_inc[(pi, 14)])
                    for half in range(2):
                        v = S.op("tensor", lambda e_, o=sc_ps[0].ap()[:, half * 512:(half + 1) * 512],
                                 l=ones_r.ap(), r=den_sb.ap()[:, half * 512:(half + 1) * 512]:
                                 e_.matmul(o, l, r, start=True, stop=True),
                                 tE if half == 1 else None)
                    r_done = v
                    S.wait("scalar", tE, r_done)
                    r_evac_inc[pi] = S.op("scalar", lambda e_, o=r_sb.ap(), i=sc_ps[0].ap():
                                          e_.copy(o, i), sE)
                    S.wait("vector", sE, r_evac_inc[pi])
                    avtt_inc[pi] = S.op("vector", lambda e_, o=qcols(h, qlo, qlo + 1024),
                                        a=av_ps.ap(), b=r_sb.ap(): e_.tensor_mul(o, a, b), vE)
                    if pi >= 8 and next_ts <= 23:
                        emit_wd_ts(next_ts)
                        next_ts += 1
                        if next_prep < 32:
                            emit_wd_prep(next_prep)
                            next_prep += 1
                while next_ts < 32:
                    emit_wd_ts(next_ts)
                    next_ts += 1
                    if next_prep < 32:
                        emit_wd_prep(next_prep)
                        next_prep += 1

                p2_vE_end = S.cur(vE)

            dbg_av_dump = 0
            if debug is True:
                S.wait("sync", vE, p2_vE_end)
                dbg_av_dump = S.dma("sync", qS, dbg_av, bb)
            with ExitStack() as ps3:
                ep3 = ps3.enter_context
                po_ps = [ep3(nc.psum_tensor(f"po_ps{i}", [128, 512], F32)) for i in range(2)]

                po_evac = {}
                s_store = [0] * DCH
                s_store_f = [0] * DCH
                g3 = 0
                for ec in range(DCH):
                    for st4 in range(4):
                        pslot = g3 % 2
                        if g3 >= 2:
                            S.wait("tensor", sE, po_evac[g3 - 2])
                        if g3 == 0:
                            S.wait("tensor", vE, p2_vE_end)
                            S.wait("tensor", fG, wout_f)
                        for fc in range(DCH):
                            last = fc == DCH - 1
                            v = S.op("tensor", lambda e_, o=po_ps[pslot].ap(),
                                     l=ba[:, fc * 2048 + ec * 128: fc * 2048 + (ec + 1) * 128],
                                     r=qcols(fc, st4 * 512, (st4 + 1) * 512),
                                     st_=(fc == 0), sp_=last:
                                     e_.matmul(o, l, r, start=st_, stop=sp_), tE if last else None)
                        S.wait("scalar", tE, v)
                        if ec >= 4 and st4 == 0:
                            S.wait("scalar", fG, s_store_f[ec - 4])
                        po_evac[g3] = S.op(
                            "scalar", lambda e_, o=s_stage[ec % 4].ap()[:, st4 * 512:(st4 + 1) * 512],
                            i=po_ps[pslot].ap(): e_.copy(o, i), sE)
                        g3 += 1
                    S.wait("gpsimd", sE, po_evac[g3 - 1])
                    s_store[ec] = S.dma("gpsimd", qG, s_sp[ec * 128:(ec + 1) * 128, :],
                                        s_stage[ec % 4].ap())
                    s_store_f[ec] = S.fwd("gpsimd", qG, s_store[ec], fG)
                p3_tE_end = S.cur(tE)
                p3_sE_end = S.cur(sE)
                p3_qG_s_last = s_store[DCH - 1]

        with ExitStack() as p4:
            e4 = p4.enter_context
            x_tile = [e4(nc.sbuf_tensor(f"x_tile{i}", [128, 2048], F32)) for i in range(2)]
            out_sb = [e4(nc.sbuf_tensor(f"out_sb{i}", [128, 2048], F32)) for i in range(3)]
            wd_buf = [e4(nc.sbuf_tensor(f"wd_buf{i}", [128, 4096], BF16)) for i in range(2)]

            with ExitStack() as ps4:
                ep4 = ps4.enter_context
                g_ps = [ep4(nc.psum_tensor(f"g_ps{i}", [128, 512], F32)) for i in range(2)]
                u_ps = [ep4(nc.psum_tensor(f"u_ps{i}", [128, 512], F32)) for i in range(2)]
                d_ps = [ep4(nc.psum_tensor(f"d_ps{i}", [128, 512], F32)) for i in range(2)]
                tr4 = [ep4(nc.psum_tensor(f"tr4_{i}", [128, 1024], BF16)) for i in range(2)]

                wg_buf = [ba[:, 0:4096], ba[:, 4096:8192]]
                wu_buf = [ba[:, 8192:12288], ba[:, 12288:16384]]
                xn_blk = ba[:, 16384:24576]
                s_blk = ba[:, 24576:32768]
                hmid = bb

                xnsp3 = xn_sp.rearrange("(c p) t -> p c t", p=128)
                ssp3 = s_sp.rearrange("(c p) t -> p c t", p=128)
                xnb3 = xn_blk.rearrange("p (c t) -> p c t", c=16)
                sb3 = s_blk.rearrange("p (c t) -> p c t", c=16)
                cb3_par = [ba[:, 0:8192].rearrange("p (c t) -> p c t", c=16),
                           ba[:, 0:8192].rearrange("p (c t) -> p c t", c=16)]

                dbg3_gates = {}
                gu_mm_done = [0] * NBLK
                tp_done = [0] * NBLK
                stt_done = {}
                comb_done = {}
                silu_done = {}
                dsilu_done = {}
                dn_grp = {}
                tt_add = {}
                out_store = {}
                gu_group = {}

                dbg_hm_dump = 0
                for b in range(NBLK):
                    if debug is True and b == 1:
                        S.wait("sync", vE, stt_done[(0, 63)])
                        S.wait("sync", tE, dn_grp[(0, 31)])
                        dbg_hm_dump = S.dma("sync", qS, dbg_hm, bb)
                    if b >= 1:
                        S.wait("sync", tE, gu_mm_done[b - 1])
                    else:
                        S.wait("sync", tE, p3_tE_end)
                    S.wait("sync", fS, tab_f)
                    xn_ld = 0
                    for hf in range(2):
                        xn_ld = S.dma("sync", qD, xnb3[:, hf * 8:(hf + 1) * 8, :],
                                      xnsp3[:, hf * 8:(hf + 1) * 8, b * 512:(b + 1) * 512])
                    xn_fwd = S.fwd("sync", qD, xn_ld, fS)
                    if b >= 1:
                        S.wait("gpsimd", tE, tp_done[b - 1])
                        if debug == 3 and 1 in dbg3_gates:
                            S.wait("gpsimd", qS, dbg3_gates[1])
                    else:
                        S.wait("gpsimd", tE, p3_tE_end)
                    S.wait("gpsimd", fG, s_store_f[DCH - 1])
                    s_ld = 0
                    for hf in range(2):
                        s_ld = S.dma("gpsimd", qH, sb3[:, hf * 8:(hf + 1) * 8, :],
                                     ssp3[:, hf * 8:(hf + 1) * 8, b * 512:(b + 1) * 512])
                    s_fwd = S.fwd("gpsimd", qH, s_ld, fG)

                    for sp in range(32):
                        slot = sp % 2
                        if sp >= 2:
                            S.wait("sync", tE, gu_group[(b, sp * 2 - 3)])
                        elif b >= 1:
                            S.wait("sync", tE, gu_mm_done[b - 1])
                            S.wait("sync", tE, tp_done[b - 1])
                            if debug == 3 and 1 in dbg3_gates:
                                S.wait("sync", qS, dbg3_gates[1])
                        else:
                            S.wait("sync", tE, p3_tE_end)
                        S.wait("sync", fG, gu_cast_f)
                        wg_ld = S.dma("sync", [qA, qA1][slot], wg_buf[slot], wg_b_d[sp])
                        wu_ld = S.dma("sync", [qB, qB1][slot], wu_buf[slot], wu_b_d[sp])
                        for ici in range(2):
                            ic = sp * 2 + ici
                            pslot = ic % 2
                            S.wait("tensor", [qA, qA1][slot], wg_ld)
                            if b == 0 and ic == 0:
                                S.wait("tensor", fS, xn_fwd)
                                S.wait("tensor", fS, consts_f)
                                S.wait("tensor", sE, p3_sE_end)
                            if ic >= 2:
                                S.wait("tensor", sE, silu_done[(b, ic - 2)])
                            elif b >= 1:
                                S.wait("tensor", sE, silu_done[(b - 1, 62 + ic)])
                            for hc in range(DCH):
                                v = S.op("tensor", lambda e_, o=g_ps[pslot].ap(),
                                         l=wg_buf[slot][:, (ici * 16 + hc) * 128:(ici * 16 + hc + 1) * 128],
                                         r=xnb3[:, hc, :], st_=(hc == 0), sp_=(hc == DCH - 1):
                                         e_.matmul(o, l, r, start=st_, stop=sp_),
                                         tE if hc == DCH - 1 else None)
                            g_done = v
                            S.wait("tensor", [qB, qB1][slot], wu_ld)
                            if ic >= 2:
                                S.wait("tensor", vE, stt_done[(b, ic - 2)])
                            elif b >= 1:
                                S.wait("tensor", vE, stt_done[(b - 1, 62 + ic)])
                            if debug and b == 1 and ic == 0:
                                S.wait("vector", qS, dbg_hm_dump)
                            for hc in range(DCH):
                                v = S.op("tensor", lambda e_, o=u_ps[pslot].ap(),
                                         l=wu_buf[slot][:, (ici * 16 + hc) * 128:(ici * 16 + hc + 1) * 128],
                                         r=xnb3[:, hc, :], st_=(hc == 0), sp_=(hc == DCH - 1):
                                         e_.matmul(o, l, r, start=st_, stop=sp_),
                                         tE if hc == DCH - 1 else None)
                            u_done = v
                            gu_group[(b, ic)] = u_done
                            S.wait("scalar", tE, g_done)
                            if ic >= 2:
                                S.wait("scalar", vE, stt_done[(b, ic - 2)])
                            elif b >= 1:
                                S.wait("scalar", vE, stt_done[(b - 1, 62 + ic)])
                            silu_done[(b, ic)] = S.op(
                                "scalar", lambda e_, o=scr4.ap()[:, pslot * 512:(pslot + 1) * 512],
                                i=g_ps[pslot].ap(), sc=gs_sb.ap()[:, ic:ic + 1]:
                                e_.activation(o, i, AF.Silu, scale=sc), sE)
                            S.wait("vector", tE, u_done)
                            S.wait("vector", sE, silu_done[(b, ic)])
                            if debug is True and b == 0 and ic == 0:
                                S.wait("vector", qS, dbg_av_dump)
                            stt_done[(b, ic)] = S.op(
                                "vector", lambda e_, o=hmid[:, ic * 512:(ic + 1) * 512],
                                i0=u_ps[pslot].ap(), sc=us_sb.ap()[:, ic:ic + 1],
                                i1=scr4.ap()[:, pslot * 512:(pslot + 1) * 512]:
                                e_.scalar_tensor_tensor(o, i0, sc, i1, ALU.mult, ALU.mult), vE)
                    gu_mm_done[b] = gu_group[(b, 63)]
                    if debug is True and b == 1:
                        pass

                    for sw in range(8):
                        for sp4 in range(4):
                            gsp = sw * 4 + sp4
                            dslot = gsp % 2
                            if gsp >= 2:
                                S.wait("sync", tE, dn_grp[(b, gsp - 2)])
                            elif b >= 1:
                                S.wait("sync", tE, dn_grp[(b - 1, 30 + gsp)])
                            else:
                                S.wait("sync", tE, p3_tE_end)
                                S.wait("sync", sE, p3_sE_end)
                                S.wait("sync", fS, wd_w_f[31])
                                S.wait("sync", fG, s_store_f[DCH - 1])
                            S.wait("sync", fS, wd_w_f[gsp])
                            wd_ld = S.dma("sync", qC, wd_buf[dslot].ap(), wd_b_d[gsp])
                            wd_fwd = S.fwd("sync", qC, wd_ld, fS)
                            S.wait("tensor", fS, wd_fwd)
                            for ici in range(16):
                                ic = sp4 * 16 + ici
                                S.wait("tensor", vE, stt_done[(b, 63)] if sw == 0 else stt_done[(b, ic)])
                                if ic == 0 and sw > 0:
                                    S.wait("tensor", sE, dsilu_done[(b, sw - 1)])
                                elif ic == 0 and b >= 1:
                                    S.wait("tensor", sE, dsilu_done[(b - 1, 7)])
                                for hcs in range(2):
                                    v = S.op("tensor", lambda e_, o=d_ps[hcs].ap(),
                                             l=wd_buf[dslot].ap()[:, (ici * 2 + hcs) * 128:(ici * 2 + hcs + 1) * 128],
                                             r=hmid[:, ic * 512:(ic + 1) * 512],
                                             st_=(ic == 0), sp_=(ic == NIC - 1):
                                             e_.matmul(o, l, r, start=st_, stop=sp_),
                                             tE if hcs == 1 else None)
                            dn_grp[(b, gsp)] = v
                        if debug == 3 and sw == 1:
                            S.wait("scalar", qS, dbg_ga_dump)
                        for hcs in range(2):
                            hc = sw * 2 + hcs
                            S.wait("scalar", tE, dn_grp[(b, sw * 4 + 3)])
                            if sw >= 1:
                                S.wait("scalar", vE, comb_done[(b, sw - 1)])
                            elif b >= 1:
                                S.wait("scalar", vE, comb_done[(b - 1, 7)])
                            sv = S.op("scalar", lambda e_, o=scr4.ap()[:, 1024 + hcs * 512: 1536 + hcs * 512],
                                      i=d_ps[hcs].ap(), sc=ds_sb.ap()[:, hc:hc + 1]:
                                      e_.activation(o, i, AF.Silu, scale=sc), sE)
                            if hcs == 1:
                                dsilu_done[(b, sw)] = sv
                            S.wait("vector", sE, sv)
                            if sw == 0 and hcs == 0:
                                S.wait("vector", fG, s_fwd)
                            comb_done[(b, sw)] = S.op(
                                "vector", lambda e_, o=cb3_par[b % 2][:, hc, :], a=sb3[:, hc, :],
                                b_=scr4.ap()[:, 1024 + hcs * 512: 1536 + hcs * 512]:
                                e_.tensor_mul(o, a, b_), vE)
                            if debug == 3 and sw == 0:
                                S.wait("sync", sE, sv)
                                dbg_ga_dump = S.dma("sync", qS, dbg_ga[b][:, hcs * 512:(hcs + 1) * 512],
                                                    scr4.ap()[:, 1024 + hcs * 512: 1536 + hcs * 512])

                    for tsub in range(4):
                        trow = b * 512 + tsub * 128
                        xslot = tsub % 2
                        prev_tt = None
                        if tsub >= 2:
                            prev_tt = tt_add.get((b, tsub - 2, 1))
                        elif b >= 1:
                            prev_tt = tt_add.get((b - 1, tsub + 2, 1))
                        if prev_tt:
                            S.wait("gpsimd", vE, prev_tt)
                        xt_ld = S.dma("gpsimd", [qE, qE1][xslot], x_tile[xslot].ap(), x_d[trow:trow + 128, :])
                        oslot = (b * 4 + tsub) % 3
                        for r in range(2):
                            S.wait("tensor", vE, comb_done[(b, 7)])
                            key_prev = (b, tsub - 1, r) if tsub >= 1 else ((b - 1, 3, r) if b >= 1 else None)
                            if key_prev in tt_add:
                                S.wait("tensor", vE, tt_add[key_prev])
                            for j in range(8):
                                hc = r * 8 + j
                                v = S.op("tensor", lambda e_, o=tr4[r].ap()[:, j * 128:(j + 1) * 128],
                                         i=cb3_par[b % 2][:, hc, tsub * 128:(tsub + 1) * 128], idn=ident.ap():
                                         e_.transpose(o, i, idn), tE if j == 7 else None)
                            tp_done[b] = v
                            S.wait("vector", tE, v)
                            S.wait("vector", [qE, qE1][xslot], xt_ld)
                            pkey = (b * 4 + tsub) - 3
                            if pkey >= 0:
                                S.wait("vector", qO, out_store[pkey])
                            tt_add[(b, tsub, r)] = S.op(
                                "vector", lambda e_, o=out_sb[oslot].ap()[:, r * 1024:(r + 1) * 1024],
                                a=tr4[r].ap(), b_=x_tile[xslot].ap()[:, r * 1024:(r + 1) * 1024]:
                                e_.tensor_add(o, a, b_), vE)
                        S.wait("gpsimd", vE, tt_add[(b, tsub, 1)])
                        out_store[b * 4 + tsub] = S.dma("gpsimd", [qO, qO1][(b * 4 + tsub) % 2],
                                                        out_d[trow:trow + 128, :], out_sb[oslot].ap())
                    if debug == 3:
                        S.wait("sync", tE, tp_done[b])
                        S.dma("sync", qS, dbg_cb[b], ba[:, 0:8192])
                        dbg_cb_dump = S.dma("sync", qS, dbg_sb[b], ba[:, 24576:32768])
                        if b == 0:
                            S.wait("sync", vE, stt_done[(0, 63)])
                            S.wait("sync", tE, dn_grp[(0, 31)])
                            dbg_hm_d3 = S.dma("sync", qS, dbg_hm, bb)
                            dbg3_gates[0] = dbg_hm_d3
                        dbg3_gates[1] = dbg_cb_dump

                S.wait("gpsimd", qE, S.cur(qE))
                S.wait("gpsimd", qE1, S.cur(qE1))
                S.wait("gpsimd", qH, S.cur(qH))
                S.wait("gpsimd", qO, S.cur(qO))
                S.wait("gpsimd", qO1, S.cur(qO1))
                S.wait("gpsimd", qG, S.cur(qG))
                S.wait("sync", qS, S.cur(qS))
                S.wait("sync", qA, S.cur(qA))
                S.wait("sync", qA1, S.cur(qA1))
                S.wait("sync", qB, S.cur(qB))
                S.wait("sync", qB1, S.cur(qB1))
                S.wait("sync", qC, S.cur(qC))
                S.wait("sync", qD, S.cur(qD))
                S.wait("sync", qX0, S.cur(qX0))
                S.wait("sync", qX1, S.cur(qX1))

        with nc.Block() as block:
            @block.sync
            def _(eng):
                for fn in S.ops["sync"]:
                    fn(eng)

            @block.gpsimd
            def _(eng):
                for fn in S.ops["gpsimd"]:
                    fn(eng)

            @block.tensor
            def _(eng):
                for fn in S.ops["tensor"]:
                    fn(eng)

            @block.vector
            def _(eng):
                for fn in S.ops["vector"]:
                    fn(eng)

            @block.scalar
            def _(eng):
                for fn in S.ops["scalar"]:
                    fn(eng)

    return nc


def _host_prep(L0_attn_x, L0_attn_w_attn, L0_attn_w_out, L0_ffn_gate_w, L0_ffn_gate_scale,
               L0_ffn_up_w, L0_ffn_up_scale, L0_ffn_down_w, L0_ffn_down_scale):
    def gu_img(w):
        a = np.ascontiguousarray(w.T)
        a = a.reshape(16, 128, 64, 128)
        a = a.transpose(2, 1, 0, 3)
        a = a.reshape(32, 2, 128, 16, 128).transpose(0, 2, 1, 3, 4)
        return np.ascontiguousarray(a.reshape(32, 128, 4096))

    def d_img(w):
        a = np.ascontiguousarray(w.T)
        a = a.reshape(4, 16, 128, 8, 2, 128)
        a = a.transpose(3, 0, 2, 1, 4, 5)
        return np.ascontiguousarray(a.reshape(32, 128, 4096))

    wg_img = gu_img(np.asarray(L0_ffn_gate_w))
    wu_img = gu_img(np.asarray(L0_ffn_up_w))
    wd_img = d_img(np.asarray(L0_ffn_down_w))

    gs = np.ascontiguousarray(np.asarray(L0_ffn_gate_scale, dtype=np.float32).reshape(64, 128).T)
    us = np.ascontiguousarray(np.asarray(L0_ffn_up_scale, dtype=np.float32).reshape(64, 128).T)
    ds = np.ascontiguousarray(np.asarray(L0_ffn_down_scale, dtype=np.float32).reshape(16, 128).T)

    jj = np.arange(0, 128, 2, dtype=np.float32)
    pos = np.float32(10000.0) ** ((np.float32(-2.0) * jj - np.float32(1.0)) / np.float32(128.0))
    tt = np.arange(T, dtype=np.float32)
    ang = (tt[:, None] * pos[None, :]).astype(np.float32)
    cos = np.cos(np.concatenate([ang, ang], axis=1)).astype(np.float32)
    sin = np.sin(np.concatenate([ang, ang], axis=1)).astype(np.float32)
    sin_s = np.concatenate([-sin[:, :64], sin[:, 64:]], axis=1)
    cos_t = np.ascontiguousarray(cos.reshape(16, 128, 128).transpose(1, 0, 2).reshape(128, T))
    sin_t = np.ascontiguousarray(sin_s.reshape(16, 128, 128).transpose(1, 0, 2).reshape(128, T))

    shared = dict(
        w_attn=np.ascontiguousarray(np.asarray(L0_attn_w_attn, dtype=np.float32)),
        w_out=np.ascontiguousarray(np.asarray(L0_attn_w_out, dtype=np.float32)),
        wg_img_i=wg_img, wu_img_i=wu_img, wd_img_i=wd_img,
        gscale=gs, uscale=us, dscale=ds,
        cos_t=cos_t, sin_t=sin_t,
        ident=np.eye(128, dtype=ml_dtypes.bfloat16),
        ones_col=np.ones((128, 1), dtype=ml_dtypes.bfloat16),
        ones_row=np.ones((1, 128), dtype=np.float32),
    )
    in_maps = []
    xa = np.asarray(L0_attn_x, dtype=np.float32)
    for c in range(8):
        m = dict(shared)
        m["x"] = np.ascontiguousarray(xa[c])
        in_maps.append(m)
    return in_maps


def kernel(**inputs):
    in_maps = _host_prep(**inputs)
    nc = build_program(debug=False)
    res = run_bass_kernel_spmd(nc, in_maps, core_ids=list(range(8)))
    return np.stack([np.asarray(r["out"], dtype=np.float32) for r in res.results], axis=0)
